# version 15
# baseline (speedup 1.0000x reference)
import numpy as np
from contextlib import ExitStack

import concourse.bass as bass
import concourse.mybir as mybir
from concourse.ap import AP
from concourse.bass_utils import run_bass_kernel_spmd

F32 = mybir.dt.float32
BF16 = mybir.dt.bfloat16
I32 = mybir.dt.int32
AF = mybir.ActivationFunctionType
ALU = mybir.AluOpType
AXX = mybir.AxisListType.X

D = 2048
T = 2048
S = 16
I0, NB0 = 5504, 43
I1, NB1 = 5632, 44
NH, NKV, HD = 16, 4, 128
DT = D // 128
ZROW = 2112
PI = 3.14159265358979323846
LN1E4 = 9.210340371976184


class Bld:
    SEMS = ("qs", "qc", "qg", "pe", "ve", "ac", "po")

    def __init__(self):
        self.nc = bass.Bass("TRN2", target_bir_lowering=False)
        self.es = ExitStack()
        self.scopes = [self.es]
        self.cnt = {}
        for name in self.SEMS:
            h = self.es.enter_context(self.nc.semaphore(name))
            setattr(self, name, h)
            self.cnt[name] = 0
        self.dram = {}
        self._uid = 0
        self.pump_q = []
        self.pump_last = None

    def uid(self):
        self._uid += 1
        return self._uid

    def inc(self, inst, name, k=1):
        inst.then_inc(getattr(self, name), k)
        self.cnt[name] += k
        return self.cnt[name]

    def sb(self, name, shape, dt):
        return self.scopes[-1].enter_context(
            self.nc.sbuf_tensor(f"{name}_{self.uid()}", shape, dt))

    def ps(self, name, shape, dt=F32):
        return self.scopes[-1].enter_context(
            self.nc.psum_tensor(f"{name}_{self.uid()}", shape, dt))

    def dr(self, name, shape, dt, kind="Internal"):
        ap = self.nc.dram_tensor(name, shape, dt, kind=kind).ap()
        self.dram[name] = ap
        return ap

    def push(self):
        self.scopes.append(ExitStack())

    def pop(self):
        self.barrier()
        self.scopes.pop().close()

    def pump(self, n=1):
        for _ in range(n):
            if not self.pump_q:
                return
            fn = self.pump_q.pop(0)
            fn()

    def pump_drain(self):
        while self.pump_q:
            self.pump_q.pop(0)()

    def pump_fence(self):
        return self.pump_last

    def barrier(self):
        snap = [(name, v) for name, v in self.cnt.items() if v > 0]
        for eng in ("sync", "scalar", "vector", "tensor", "gpsimd"):
            def mk(e, snap=snap):
                for name, v in snap:
                    e.wait_ge(getattr(self, name), v)
            getattr(self.block, eng)(mk)



def load_vec_cols(b, dst, src_vec, n_chunks):
    src = AP(src_vec.tensor, 0, [[1, 128], [128, n_chunks]])

    @b.block.sync
    def _(q):
        b.inc(q.dma_start(out=dst[:, 0:n_chunks], in_=src,
                          allow_slow_non_contiguous=True), "qs", 16)


def mm_grid(b, nmb, ntc, nkt, lhsT_fn, rhs_fn, evac_fn, pre_wait=None, pump_per=0):
    b.push()
    psums = [b.ps("mmg", [128, 512]) for _ in range(2)]
    evac_ms = []
    n = 0
    for mb in range(nmb):
        for tc in range(ntc):
            pj = n % 2

            @b.block.tensor
            def _(t, n=n, pj=pj, mb=mb, tc=tc):
                if n == 0 and pre_wait is not None:
                    for name, v in pre_wait:
                        t.wait_ge(getattr(b, name), v)
                if n >= 2:
                    sname, v = evac_ms[n - 2]
                    t.wait_ge(getattr(b, sname), v)
                last = None
                for kt in range(nkt):
                    last = t.matmul(out=psums[pj][:], lhsT=lhsT_fn(kt, mb),
                                    rhs=rhs_fn(kt, tc),
                                    start=(kt == 0), stop=(kt == nkt - 1))
                b.inc(last, "pe")

            evac_ms.append(evac_fn(mb, tc, psums[pj]))
            b.pump(pump_per)
            n += 1
    b.pop()


def stream_stripes_T(b, dst_list, d_rows, queue="qs"):
    fence = b.pump_fence()

    @b.block.sync
    def _(q):
        if fence is not None:
            q.wait_ge(getattr(b, fence[0]), fence[1])
        for k, dst in enumerate(dst_list):
            b.inc(q.dma_start_transpose(dst[:], d_rows[:, k * 128:(k + 1) * 128]),
                  "qs", 16)


def load_stripes(b, dst_list, d_rows):
    rt = d_rows.rearrange("(n p) c -> n p c", p=128)

    @b.block.sync
    def _(q):
        for k, dst in enumerate(dst_list):
            b.inc(q.dma_start(out=dst[:], in_=rt[k]), "qs", 16)


def cast_load_stripes(b, dst_list, src_hbm):
    st = src_hbm.rearrange("(n p) c -> n p c", p=128)

    @b.block.gpsimd
    def _(g):
        for k, dst in enumerate(dst_list):
            b.inc(g.dma_start(out=dst[:], in_=st[k]), "qg", 16)



def _emit_select(b, eng, semname, src, dst, dst2, tmp, tmp2, lut128, mode,
                 scol_ap=None, smcol_ap=None):
    sem = getattr(b, semname)
    if mode == "select":
        b.inc(eng.tensor_scalar(out=dst, in0=src, scalar1=0.0,
                                scalar2=lut128[:, 0:1],
                                op0=ALU.is_equal, op1=ALU.mult), semname)
        b.inc(eng.tensor_scalar(out=dst2, in0=src, scalar1=1.0,
                                scalar2=lut128[:, 1:2],
                                op0=ALU.is_equal, op1=ALU.mult), semname)
        for vv in range(2, 16, 2):
            t0, t1 = (tmp, tmp2) if (vv // 2) % 2 == 0 else (tmp2, tmp)
            b.inc(eng.tensor_scalar(out=t0, in0=src, scalar1=float(vv),
                                    scalar2=lut128[:, vv:vv + 1],
                                    op0=ALU.is_equal, op1=ALU.mult), semname)
            b.inc(eng.tensor_scalar(out=t1, in0=src, scalar1=float(vv + 1),
                                    scalar2=lut128[:, vv + 1:vv + 2],
                                    op0=ALU.is_equal, op1=ALU.mult), semname)
            b.inc(eng.tensor_add(out=dst, in0=dst, in1=t0), semname)
            b.inc(eng.tensor_add(out=dst2, in0=dst2, in1=t1), semname)
        eng.wait_ge(sem, b.cnt[semname])
        b.inc(eng.tensor_add(out=dst, in0=dst, in1=dst2), semname)
        eng.wait_ge(sem, b.cnt[semname])
    elif mode == "dequant":
        b.inc(eng.tensor_scalar(out=dst, in0=src, scalar1=scol_ap,
                                scalar2=smcol_ap, op0=ALU.mult, op1=ALU.add),
              semname)
        eng.wait_ge(sem, b.cnt[semname])
    else:
        b.inc(eng.tensor_copy(out=dst, in_=src), semname)
        eng.wait_ge(sem, b.cnt[semname])


def prep_rows(b, name, src_hbm, R, C, mode, lut128=None, scol=None, smcol=None):
    nr = R // 128
    d_rows = b.dr(f"d_{name}_rows", [R, C], BF16)
    b.push()
    ia = [b.sb(f"{name}_i", [128, C], BF16) for _ in range(2)]
    wa = [b.sb(f"{name}_w", [128, C], BF16) for _ in range(2)]
    ta = [b.sb(f"{name}_t", [128, C], BF16) for _ in range(2)]
    wb2 = [b.sb(f"{name}_w2", [128, C], BF16) for _ in range(2)]
    tb2 = [b.sb(f"{name}_t2", [128, C], BF16) for _ in range(2)]
    src_t = src_hbm.rearrange("(n p) c -> n p c", p=128)
    dst_t = d_rows.rearrange("(n p) c -> n p c", p=128)
    v_store = []
    p_store = []

    use_pool = False
    vtiles = [i for i in range(nr) if not use_pool or i % 3 != 2]
    ptiles = [i for i in range(nr) if use_pool and i % 3 == 2]
    vpos = {i: k for k, i in enumerate(vtiles)}
    ppos = {i: k for k, i in enumerate(ptiles)}

    for i in range(nr):
        if i in vpos:
            k = vpos[i]
            j = k % 2

            @b.block.gpsimd
            def _(g, i=i, k=k, j=j):
                if k >= 2:
                    sname, v = v_store[k - 2]
                    g.wait_ge(getattr(b, sname), v)
                b.inc(g.dma_start(out=ia[j][:], in_=src_t[i]), "qg", 16)
            ld = ("qg", b.cnt["qg"])

            @b.block.vector
            def _(v, i=i, k=k, j=j, ld=ld):
                v.wait_ge(getattr(b, ld[0]), ld[1])
                if k >= 2:
                    sname, val = v_store[k - 2]
                    v.wait_ge(getattr(b, sname), val)
                sc = scol[:, i:i + 1] if scol is not None else None
                sm = smcol[:, i:i + 1] if smcol is not None else None
                _emit_select(b, v, "ve", ia[j][:], wa[j][:], wb2[j][:],
                             ta[j][:], tb2[j][:], lut128, mode, sc, sm)
            cm = ("ve", b.cnt["ve"])

            @b.block.sync
            def _(q, i=i, j=j, cm=cm):
                q.wait_ge(getattr(b, cm[0]), cm[1])
                b.inc(q.dma_start(out=dst_t[i], in_=wa[j][:]), "qs", 16)
            v_store.append(("qs", b.cnt["qs"]))
    b.pop()
    return d_rows


def queue_selects(b, tag, specs, lut128=None, scol_map=None, chalf=None,
                  ident=None):
    cmax = max(sp[3] for sp in specs)
    ia = [b.sb(f"{tag}_i", [128, cmax], BF16) for _ in range(2)]
    wa = [b.sb(f"{tag}_w", [128, cmax], BF16) for _ in range(2)]
    any_sel = any(sp[4] == "select" for sp in specs)
    if any_sel:
        mbuf = [b.sb(f"{tag}_m", [128, 1024], BF16) for _ in range(6)]
        ps_sel = b.ps(f"{tag}_ps", [128, 1024])
    sel_ms = []
    state = {"g": 0}
    rows = {}

    for name, src_hbm, R, C, mode in specs:
        d_rows = b.dr(f"d_{name}_rows", [R, C], BF16)
        rows[name] = d_rows
        scol = scol_map.get(name) if scol_map else None
        segs = [(c0, min(c0 + 1024, C)) for c0 in range(0, C, 1024)]
        for i in range(R // 128):
            def emit(i=i, name=name, src_hbm=src_hbm, R=R, C=C, mode=mode,
                     scol=scol, d_rows=d_rows, segs=segs):
                g_ = state["g"]
                j = g_ % 2
                src = AP(src_hbm.tensor, i * 128 * C, [[C, 128], [1, C]])
                dst = AP(d_rows.tensor, i * 128 * C, [[C, 128], [1, C]])

                @b.block.gpsimd
                def _(g, g_=g_, j=j):
                    if g_ >= 2:
                        sname, v = sel_ms[g_ - 2]
                        g.wait_ge(getattr(b, sname), v)
                    g.wait_ge(b.qs, b.cnt["qs"])
                    b.inc(g.dma_start(out=ia[j][:, 0:C], in_=src), "qg", 16)
                ld = ("qg", b.cnt["qg"])

                if mode == "select":
                    ve0 = b.cnt["ve"]
                    pe0 = b.cnt["pe"]
                    ac0 = b.cnt["ac"]

                    @b.block.vector
                    def _(v, g_=g_, j=j, ld=ld, ve0=ve0, pe0=pe0, segs=segs):
                        v.wait_ge(getattr(b, ld[0]), ld[1])
                        n = 0
                        for si, (c0, c1) in enumerate(segs):
                            for vv in range(16):
                                if n >= 6:
                                    v.wait_ge(b.pe, pe0 + n - 5)
                                b.inc(v.tensor_scalar(
                                    out=mbuf[n % 6][:, 0:c1 - c0],
                                    in0=ia[j][:, c0:c1], scalar1=float(vv),
                                    scalar2=lut128[:, vv:vv + 1],
                                    op0=ALU.is_equal, op1=ALU.mult), "ve")
                                n += 1

                    @b.block.tensor
                    def _(t, ve0=ve0, pe0=pe0, ac0=ac0, segs=segs):
                        n = 0
                        for si, (c0, c1) in enumerate(segs):
                            if si >= 1:
                                t.wait_ge(b.ac, ac0 + si)
                            w = c1 - c0
                            for vv in range(16):
                                t.wait_ge(b.ve, ve0 + n + 1)
                                last = None
                                for mc in range(0, w, 512):
                                    last = t.matmul(
                                        out=ps_sel[:, mc:min(mc + 512, w)],
                                        lhsT=ident[:],
                                        rhs=mbuf[n % 6][:, mc:min(mc + 512, w)],
                                        start=(vv == 0), stop=(vv == 15))
                                b.inc(last, "pe")
                                n += 1

                    @b.block.scalar
                    def _(sc, j=j, pe0=pe0, segs=segs):
                        for si, (c0, c1) in enumerate(segs):
                            sc.wait_ge(b.pe, pe0 + (si + 1) * 16)
                            b.inc(sc.activation(out=wa[j][:, c0:c1],
                                                in_=ps_sel[:, 0:c1 - c0],
                                                func=AF.Copy), "ac")
                    cm = ("ac", b.cnt["ac"])
                else:
                    @b.block.vector
                    def _(v, j=j, ld=ld, scol=scol, i=i, C=C):
                        v.wait_ge(getattr(b, ld[0]), ld[1])
                        sc = scol[0][:, i:i + 1]
                        sm = scol[1][:, i:i + 1]
                        b.inc(v.tensor_scalar(out=wa[j][:, 0:C], in0=ia[j][:, 0:C],
                                              scalar1=sc, scalar2=sm,
                                              op0=ALU.mult, op1=ALU.add), "ve")
                        v.wait_ge(b.ve, b.cnt["ve"])
                    cm = ("ve", b.cnt["ve"])

                @b.block.sync
                def _(q, j=j, cm=cm, C=C, dst=dst):
                    q.wait_ge(getattr(b, cm[0]), cm[1])
                    b.inc(q.dma_start(out=dst, in_=wa[j][:, 0:C]), "qs", 16)
                sel_ms.append(("qs", b.cnt["qs"]))
                b.pump_last = sel_ms[-1]
                state["g"] += 1
            b.pump_q.append(emit)
    return rows


def emit_tilings(b, tag, items, barrier=True):
    rmax = max(it[2] for it in items)
    if barrier:
        b.push()
    st = [b.sb(f"{tag}_s", [128, rmax], BF16) for _ in range(2)]
    til_ms = []
    out = {}
    for name, d_rows, R, C in items:
        nkt = C // 128
        nmb = R // 128
        d_til = b.dr(f"d_{name}T", [nmb, 128, nkt, 128], BF16)
        out[name] = d_til
        for kt in range(nkt):
            tj = len(til_ms) % 2

            @b.block.scalar
            def _(q, kt=kt, tj=tj, d_rows=d_rows, R=R, nt=len(til_ms)):
                if nt >= 2:
                    sname, v = til_ms[nt - 2]
                    q.wait_ge(getattr(b, sname), v)
                b.inc(q.dma_start_transpose(st[tj][:, 0:R],
                                            d_rows[:, kt * 128:(kt + 1) * 128]),
                      "qc", 16)
            tl = b.cnt["qc"]

            @b.block.sync
            def _(q, kt=kt, tj=tj, d_til=d_til, R=R, nkt=nkt, nmb=nmb, tl=tl):
                q.wait_ge(b.qc, tl)
                dst = AP(d_til.tensor, kt * 128,
                         [[nkt * 128, 128], [128 * nkt * 128, nmb], [1, 128]])
                b.inc(q.dma_start(
                    out=dst,
                    in_=st[tj].ap()[:, 0:R].rearrange("p (mb m) -> p mb m", m=128)),
                    "qs", 16)
            til_ms.append(("qs", b.cnt["qs"]))
    if barrier:
        b.pop()
    out["_ms"] = til_ms[-1] if til_ms else None
    return out


def prep_ffn_weights(b, tag, specs, lut128=None, scol_map=None):
    cmax = max(sp[3] for sp in specs)
    rmax = max(sp[2] for sp in specs)
    b.push()
    ia = [b.sb(f"{tag}_i", [128, cmax], BF16) for _ in range(2)]
    wa = [b.sb(f"{tag}_w", [128, cmax], BF16) for _ in range(2)]
    wb2 = [b.sb(f"{tag}_w2", [128, cmax], BF16) for _ in range(2)]
    ta = b.sb(f"{tag}_t", [128, cmax], BF16)
    tb2 = b.sb(f"{tag}_t2", [128, cmax], BF16)
    st = [b.sb(f"{tag}_s", [128, rmax], BF16) for _ in range(2)]
    sel_ms = []
    til_ms = []
    out = {}
    rows_of = {}
    gtile = 0

    for name, src_hbm, R, C, mode in specs:
        nr = R // 128
        d_rows = b.dr(f"d_{name}_rows", [R, C], BF16)
        rows_of[name] = d_rows
        src_t = src_hbm.rearrange("(n p) c -> n p c", p=128)
        dst_t = d_rows.rearrange("(n p) c -> n p c", p=128)
        scol = scol_map.get(name) if scol_map else None
        for i in range(nr):
            j = gtile % 2

            @b.block.gpsimd
            def _(g, i=i, j=j, gtile=gtile, src_t=src_t, C=C):
                if gtile >= 2:
                    sname, v = sel_ms[gtile - 2]
                    g.wait_ge(getattr(b, sname), v)
                b.inc(g.dma_start(out=ia[j][:, 0:C], in_=src_t[i]), "qg", 16)
            ld = ("qg", b.cnt["qg"])

            @b.block.vector
            def _(v, i=i, j=j, gtile=gtile, ld=ld, C=C, mode=mode, scol=scol):
                v.wait_ge(getattr(b, ld[0]), ld[1])
                if gtile >= 2:
                    sname, val = sel_ms[gtile - 2]
                    v.wait_ge(getattr(b, sname), val)
                sc = scol[0][:, i:i + 1] if scol is not None else None
                sm = scol[1][:, i:i + 1] if scol is not None else None
                _emit_select(b, v, "ve", ia[j][:, 0:C], wa[j][:, 0:C],
                             wb2[j][:, 0:C], ta[:, 0:C], tb2[:, 0:C],
                             lut128, mode, sc, sm)
            cm = ("ve", b.cnt["ve"])

            @b.block.sync
            def _(q, i=i, j=j, cm=cm, dst_t=dst_t, C=C):
                q.wait_ge(getattr(b, cm[0]), cm[1])
                b.inc(q.dma_start(out=dst_t[i], in_=wa[j][:, 0:C]), "qs", 16)
            sel_ms.append(("qs", b.cnt["qs"]))
            gtile += 1

    b.barrier()

    for name, src_hbm, R, C, mode in specs:
        d_rows = rows_of[name]
        nkt = C // 128
        nmb = R // 128
        d_til = b.dr(f"d_{name}T", [nmb, 128, nkt, 128], BF16)
        out[name] = d_til
        for kt in range(nkt):
            tj = len(til_ms) % 2

            @b.block.scalar
            def _(q, kt=kt, tj=tj, d_rows=d_rows, R=R, nt=len(til_ms)):
                if nt >= 2:
                    sname, v = til_ms[nt - 2]
                    q.wait_ge(getattr(b, sname), v)
                b.inc(q.dma_start_transpose(st[tj][:, 0:R],
                                            d_rows[:, kt * 128:(kt + 1) * 128]),
                      "qc", 16)
            tl = b.cnt["qc"]

            @b.block.sync
            def _(q, kt=kt, tj=tj, d_til=d_til, R=R, nkt=nkt, nmb=nmb, tl=tl):
                q.wait_ge(b.qc, tl)
                dst = AP(d_til.tensor, kt * 128,
                         [[nkt * 128, 128], [128 * nkt * 128, nmb], [1, 128]])
                b.inc(q.dma_start(
                    out=dst,
                    in_=st[tj].ap()[:, 0:R].rearrange("p (mb m) -> p mb m", m=128)),
                    "qs", 16)
            til_ms.append(("qs", b.cnt["qs"]))

    b.pop()
    return out


def prep_tiled(b, name, d_rows, R, C):
    nkt = C // 128
    nmb = R // 128
    d_til = b.dr(f"d_{name}T", [nmb, 128, nkt, 128], BF16)
    b.push()
    st = [b.sb(f"{name}_s", [128, R], BF16) for _ in range(2)]
    store_ms = []
    for kt in range(nkt):
        j = kt % 2

        @b.block.scalar
        def _(q, kt=kt, j=j):
            if kt >= 2:
                sname, v = store_ms[kt - 2]
                q.wait_ge(getattr(b, sname), v)
            b.inc(q.dma_start_transpose(st[j][:], d_rows[:, kt * 128:(kt + 1) * 128]),
                  "qc", 16)
            q.wait_ge(b.qc, b.cnt["qc"])
            dst = AP(d_til.tensor, kt * 128,
                     [[nkt * 128, 128], [128 * nkt * 128, nmb], [1, 128]])
            b.inc(q.dma_start(out=dst,
                              in_=st[j].ap().rearrange("p (mb m) -> p mb m", m=128)),
                  "qc", 16)
        store_ms.append(("qc", b.cnt["qc"]))
    b.pop()
    return d_til


def emit_fm_copy(b, name, d_rows, R, C, d_dst):
    b.push()
    st = [b.sb(f"{name}_s", [128, R], BF16) for _ in range(2)]
    dst_t = d_dst.rearrange("(n p) c -> n p c", p=128)
    store_ms = []
    fence = b.pump_fence()
    for kt in range(C // 128):
        j = kt % 2

        @b.block.scalar
        def _(q, kt=kt, j=j, fence=fence):
            if kt == 0 and fence is not None:
                q.wait_ge(getattr(b, fence[0]), fence[1])
            if kt >= 2:
                sname, v = store_ms[kt - 2]
                q.wait_ge(getattr(b, sname), v)
            b.inc(q.dma_start_transpose(st[j][:], d_rows[:, kt * 128:(kt + 1) * 128]),
                  "qc", 16)
            q.wait_ge(b.qc, b.cnt["qc"])
            b.inc(q.dma_start(out=dst_t[kt], in_=st[j][:]), "qc", 16)
        store_ms.append(("qc", b.cnt["qc"]))
    b.pop()



def emit_resid_ln(b, tag, x_src, u_src, x_dst, d_xn, pump=0):
    b.push()
    nt = T // 128
    xs = [b.sb(f"{tag}_x", [128, D], F32) for _ in range(2)]
    us = [b.sb(f"{tag}_u", [128, D], BF16) for _ in range(2)]
    xns = [b.sb(f"{tag}_xn", [128, D], BF16) for _ in range(2)]
    trash = b.sb(f"{tag}_tr", [128, D], F32)
    mcol = b.sb(f"{tag}_m", [128, 1], F32)
    vcol = b.sb(f"{tag}_v", [128, 1], F32)
    rcol = b.sb(f"{tag}_r", [128, 1], F32)
    scol = b.sb(f"{tag}_s", [128, 1], F32)
    x_t = x_src.rearrange("(n p) c -> n p c", p=128)
    xd_t = x_dst.rearrange("(n p) c -> n p c", p=128) if x_dst is not None else None
    xn_t = d_xn.rearrange("(n p) c -> n p c", p=128) if d_xn is not None else None
    store_ms = []

    for i in range(nt):
        j = i % 2

        @b.block.sync
        def _(q, i=i, j=j):
            if i >= 2:
                sname, v = store_ms[i - 2]
                q.wait_ge(getattr(b, sname), v)
            b.inc(q.dma_start(out=xs[j][:], in_=x_t[i]), "qs", 16)
            if u_src is not None:
                b.inc(q.dma_start_transpose(us[j][:],
                                            u_src[:, i * 128:(i + 1) * 128]), "qs", 16)
        ld = b.cnt["qs"]

        if u_src is not None:
            @b.block.vector
            def _(v, j=j, ld=ld):
                v.wait_ge(b.qs, ld)
                b.inc(v.tensor_add(out=xs[j][:], in0=xs[j][:], in1=us[j][:]), "ve")
                v.wait_ge(b.ve, b.cnt["ve"])
        xready = ("ve", b.cnt["ve"]) if u_src is not None else ("qs", ld)

        if d_xn is not None:
            @b.block.vector
            def _(v, j=j, xready=xready):
                v.wait_ge(getattr(b, xready[0]), xready[1])
                b.inc(v.tensor_reduce(out=mcol[:], in_=xs[j][:], axis=AXX, op=ALU.add),
                      "ve")
                v.wait_ge(b.ve, b.cnt["ve"])
                b.inc(v.tensor_scalar_mul(out=mcol[:], in0=mcol[:], scalar1=1.0 / D),
                      "ve")
                v.wait_ge(b.ve, b.cnt["ve"])
                b.inc(v.tensor_tensor(out=scol[:], in0=mcol[:], in1=mcol[:],
                                      op=ALU.mult), "ve")
                v.wait_ge(b.ve, b.cnt["ve"])

            @b.block.scalar
            def _(s, j=j, xready=xready):
                s.wait_ge(getattr(b, xready[0]), xready[1])
                b.inc(s.activation(out=trash[:], in_=xs[j][:], func=AF.Square,
                                   accum_out=vcol[:]), "ac")
            sq_ms = b.cnt["ac"]

            @b.block.vector
            def _(v, sq_ms=sq_ms):
                v.wait_ge(b.ac, sq_ms)
                b.inc(v.tensor_scalar(out=vcol[:], in0=vcol[:], scalar1=1.0 / D,
                                      scalar2=1e-5, op0=ALU.mult, op1=ALU.add), "ve")
                v.wait_ge(b.ve, b.cnt["ve"])
                b.inc(v.tensor_sub(out=vcol[:], in0=vcol[:], in1=scol[:]), "ve")
                v.wait_ge(b.ve, b.cnt["ve"])
            var_ms = b.cnt["ve"]

            @b.block.scalar
            def _(s, var_ms=var_ms):
                s.wait_ge(b.ve, var_ms)
                b.inc(s.activation(out=rcol[:], in_=vcol[:], func=AF.Ln), "ac")
                s.wait_ge(b.ac, b.cnt["ac"])
                b.inc(s.activation(out=rcol[:], in_=rcol[:], func=AF.Exp, scale=-0.5),
                      "ac")
            rstd_ms = b.cnt["ac"]

            @b.block.vector
            def _(v, j=j, rstd_ms=rstd_ms):
                v.wait_ge(b.ac, rstd_ms)
                b.inc(v.tensor_scalar(out=xns[j][:], in0=xs[j][:], scalar1=mcol[:],
                                      scalar2=rcol[:], op0=ALU.subtract, op1=ALU.mult),
                      "ve")

        done = ("ve", b.cnt["ve"]) if (d_xn is not None or u_src is not None) \
            else ("qs", ld)

        @b.block.gpsimd
        def _(g, i=i, j=j, done=done):
            g.wait_ge(getattr(b, done[0]), done[1])
            if x_dst is not None:
                b.inc(g.dma_start(out=xd_t[i], in_=xs[j][:]), "qg", 16)
            if d_xn is not None:
                b.inc(g.dma_start(out=xn_t[i], in_=xns[j][:]), "qg", 16)
        store_ms.append(("qg", b.cnt["qg"]))
        b.pump(pump)

    b.pop()



def emit_proj_in(b, d_w_rows, d_xn, bias_cols, d_out):
    b.push()
    wstr = [b.sb("pi_w", [128, D], BF16) for _ in range(DT)]
    xstr = [b.sb("pi_x", [128, T], BF16) for _ in range(DT)]
    os_ = [b.sb("pi_o", [128, 512], BF16) for _ in range(2)]
    o_t = d_out.rearrange("(n p) c -> n p c", p=128)
    stream_stripes_T(b, wstr, d_w_rows)
    load_stripes(b, xstr, d_xn)
    lw = ("qs", b.cnt["qs"])

    def evac(mb, tc, psum):
        oj = (mb * 4 + tc) % 2

        @b.block.vector
        def _(v, mb=mb, tc=tc, oj=oj, psum=psum, pe=b.cnt["pe"]):
            v.wait_ge(b.pe, pe)
            if mb * 4 + tc >= 2:
                v.wait_ge(b.qg, b.cnt["qg"] - 16)
            b.inc(v.tensor_scalar_add(out=os_[oj][:], in0=psum[:],
                                      scalar1=bias_cols[:, mb:mb + 1]), "ve")

        @b.block.gpsimd
        def _(g, mb=mb, tc=tc, oj=oj, ve=b.cnt["ve"]):
            g.wait_ge(b.ve, ve)
            b.inc(g.dma_start(out=o_t[mb][:, tc * 512:(tc + 1) * 512], in_=os_[oj][:]),
                  "qg", 16)
        return ("ve", b.cnt["ve"])

    mm_grid(b, DT, 4, DT,
            lambda kt, mb: wstr[kt][:, mb * 128:(mb + 1) * 128],
            lambda kt, tc: xstr[kt][:, tc * 512:(tc + 1) * 512],
            evac, pre_wait=[lw], pump_per=1)
    b.pop()


def emit_proj_out(b, d_w_rows, d_y_rows, bias_cols, d_xn, d_out):
    b.push()
    wstr = [b.sb("po_w", [128, D], BF16) for _ in range(DT)]
    ystr = [b.sb("po_y", [128, T], BF16) for _ in range(DT)]
    xn_t = [b.sb("po_xn", [128, T], BF16) for _ in range(2)]
    os_ = [b.sb("po_o", [128, 512], BF16) for _ in range(2)]
    o_t = d_out.rearrange("(n p) c -> n p c", p=128)
    stream_stripes_T(b, wstr, d_w_rows)
    load_stripes(b, ystr, d_y_rows)
    lw = ("qs", b.cnt["qs"])
    xn_ready = [None, None]
    xn_free = {}

    def evac(mb, tc, psum):
        n = mb * 4 + tc
        oj = n % 2
        if tc == 0:
            @b.block.scalar
            def _(q, mb=mb):
                if mb >= 2:
                    sname, v = xn_free[mb - 2]
                    q.wait_ge(getattr(b, sname), v)
                b.inc(q.dma_start(
                    out=xn_t[mb % 2][:],
                    in_=d_xn.rearrange("(a p) c -> a p c", p=128)[mb]), "qc", 16)
            ld = b.cnt["qc"]

            @b.block.vector
            def _(v, mb=mb, ld=ld):
                v.wait_ge(b.qc, ld)
                b.inc(v.tensor_scalar_add(out=xn_t[mb % 2][:], in0=xn_t[mb % 2][:],
                                          scalar1=bias_cols[:, mb:mb + 1]), "ve")
                v.wait_ge(b.ve, b.cnt["ve"])

        @b.block.vector
        def _(v, mb=mb, tc=tc, oj=oj, psum=psum, pe=b.cnt["pe"], n=n):
            v.wait_ge(b.pe, pe)
            if n >= 2:
                v.wait_ge(b.qg, b.cnt["qg"] - 16)
            b.inc(v.tensor_add(out=os_[oj][:],
                               in0=xn_t[mb % 2][:, tc * 512:(tc + 1) * 512],
                               in1=psum[:]), "ve")
        ms = ("ve", b.cnt["ve"])
        if tc == 3:
            xn_free[mb] = ms

        @b.block.gpsimd
        def _(g, mb=mb, tc=tc, oj=oj, ve=b.cnt["ve"]):
            g.wait_ge(b.ve, ve)
            b.inc(g.dma_start(out=o_t[mb][:, tc * 512:(tc + 1) * 512], in_=os_[oj][:]),
                  "qg", 16)
        return ms

    mm_grid(b, DT, 4, DT,
            lambda kt, mb: wstr[kt][:, mb * 128:(mb + 1) * 128],
            lambda kt, tc: ystr[kt][:, tc * 512:(tc + 1) * 512],
            evac, pre_wait=[lw], pump_per=1)
    b.pop()



def emit_scan(b, bw_in, a_in, cw_in, dp_cols, onesr, d_bw_rows, d_xp, d_zpad, d_y0):
    b.push()
    nc = b.nc
    xstr = [b.sb("sc_x", [128, T], BF16) for _ in range(DT)]
    bwT = [b.sb("sc_bwT", [128, S], BF16) for _ in range(DT)]
    ident = b.sb("sc_id", [128, 128], BF16)
    identf = b.sb("sc_idf", [128, 128], F32)
    pcolf = b.sb("sc_pcol", [128, 1], F32)
    i16a = b.sb("sc_i16a", [16, 128], F32)
    delta_bf = b.sb("sc_deltab", [16, T], BF16)
    dl1 = b.sb("sc_dl1", [16, 512], F32)
    dl2 = b.sb("sc_dl2", [16, 512], F32)
    dl3 = b.sb("sc_dl3", [16, 512], F32)
    ones16 = b.sb("sc_ones16", [16, 1], F32)
    r16 = b.sb("sc_r16", [16, 128], BF16)
    r16f = b.sb("sc_r16f", [16, 128], F32)
    drep = b.sb("sc_drep", [128, T], F32)
    acol = b.sb("sc_acol", [128, DT * S], F32)
    lna = b.sb("sc_lna", [128, DT * S], F32)
    eiota = b.sb("sc_ei", [128, 512], F32)
    eio2 = b.sb("sc_ei2", [128, 512], F32)
    ptmp = [b.sb("sc_ptmp", [128, 512], F32) for _ in range(2)]
    pstr = b.sb("sc_pstr", [128, DT * 512], BF16)
    zt = [b.sb("sc_zt", [128, 512], BF16) for _ in range(2)]
    zhead = b.sb("sc_zh", [128, 31], BF16)
    zsh = [b.sb("sc_zsh", [128, T], BF16) for _ in range(4)]
    cw_fm = b.sb("sc_cwfm", [128, S], BF16)
    cwT = b.sb("sc_cwT", [16, T], BF16)
    cwrep = b.sb("sc_cwrep", [128, T], BF16)
    ys = [b.sb("sc_y", [128, 512], BF16) for _ in range(2)]
    ps_tb = b.ps("sc_pstb", [16, 128], BF16)
    ps_r = b.ps("sc_psr", [128, 512])
    ps_d = ps_r[0:16, :]
    y_t = d_y0.rearrange("(n p) c -> n p c", p=128)
    xp_t = d_xp.rearrange("(n p) c -> n p c", p=128)

    load_stripes(b, xstr, d_xp)
    stream_stripes_T(b, bwT, d_bw_rows)
    xp_ld = b.cnt["qs"]

    @b.block.gpsimd
    def _(g):
        g.iota(i16a[:], pattern=[[0, 8], [1, 16]], base=0, channel_multiplier=0,
               allow_small_or_imprecise_dtypes=True)
        g.iota(eiota[:], pattern=[[8, 4], [0, 8], [0, 16]], base=7,
               channel_multiplier=0, allow_small_or_imprecise_dtypes=True)
        g.iota(eio2[:], pattern=[[0, 4], [1, 8], [0, 16]], base=0,
               channel_multiplier=0, allow_small_or_imprecise_dtypes=True)
        i = g.iota(pcolf[:], pattern=[[0, 1]], base=0, channel_multiplier=1,
                   allow_small_or_imprecise_dtypes=True)
        b.inc(i, "po")

    @b.block.vector
    def _(v, po=b.cnt["po"]):
        v.wait_ge(b.po, po)
        b.inc(v.memset(ones16[:], 1.0), "ve")
        b.inc(v.memset(zhead[:], 0.0), "ve")
        b.inc(v.tensor_sub(out=eiota[:], in0=eiota[:], in1=eio2[:]), "ve")

    idcol = b.sb("sc_idcol", [128, 128], F32)

    @b.block.gpsimd
    def _(g):
        i = g.iota(idcol[:], pattern=[[1, 128]], base=0, channel_multiplier=0,
                   allow_small_or_imprecise_dtypes=True)
        b.inc(i, "po")

    @b.block.vector
    def _(v, po=b.cnt["po"]):
        v.wait_ge(b.po, po)
        b.inc(v.tensor_scalar(out=identf[:], in0=idcol[:], scalar1=pcolf[:],
                              scalar2=None, op0=ALU.is_equal), "ve")
        v.wait_ge(b.ve, b.cnt["ve"])
        b.inc(v.tensor_copy(out=ident[:], in_=identf[:]), "ve")
        b.inc(v.tensor_scalar(out=r16f[:], in0=i16a[:], scalar1=pcolf[0:16, 0:1],
                              scalar2=None, op0=ALU.is_equal), "ve")
        v.wait_ge(b.ve, b.cnt["ve"])
        b.inc(v.tensor_copy(out=r16[:], in_=r16f[:]), "ve")
    consts_ms = ("ve", b.cnt["ve"])

    for tc in range(4):
        @b.block.tensor
        def _(t, tc=tc, ve=b.cnt["ve"]):
            t.wait_ge(b.qs, xp_ld)
            if tc >= 1:
                t.wait_ge(b.ve, ve)
            last = None
            for kt in range(DT):
                last = t.matmul(out=ps_d, lhsT=bwT[kt][:],
                                rhs=xstr[kt][:, tc * 512:(tc + 1) * 512],
                                start=(kt == 0), stop=(kt == DT - 1))
            b.inc(last, "pe")

        @b.block.vector
        def _(v, tc=tc, pe=b.cnt["pe"], cms=consts_ms):
            v.wait_ge(b.pe, pe)
            v.wait_ge(getattr(b, cms[0]), cms[1])
            b.inc(v.tensor_relu(out=dl2[:], in_=ps_d), "ve")
            v.wait_ge(b.ve, b.cnt["ve"])
            b.inc(v.scalar_tensor_tensor(out=dl1[:], in0=dl2[:], scalar=2.0,
                                         in1=ps_d, op0=ALU.mult,
                                         op1=ALU.subtract), "ve")
            v.wait_ge(b.ve, b.cnt["ve"])

        @b.block.scalar
        def _(s, tc=tc, ve=b.cnt["ve"]):
            s.wait_ge(b.ve, ve)
            b.inc(s.activation(out=dl3[:], in_=dl1[:], func=AF.Exp, scale=-1.0), "ac")
            s.wait_ge(b.ac, b.cnt["ac"])
            b.inc(s.activation(out=dl3[:], in_=dl3[:], func=AF.Ln, bias=ones16[:]),
                  "ac")

        @b.block.vector
        def _(v, tc=tc, ac=b.cnt["ac"]):
            v.wait_ge(b.ac, ac)
            b.inc(v.tensor_add(out=dl2[:], in0=dl2[:], in1=dl3[:]), "ve")
            v.wait_ge(b.ve, b.cnt["ve"])
            b.inc(v.tensor_copy(out=delta_bf[:, tc * 512:(tc + 1) * 512], in_=dl2[:]),
                  "ve")
            v.wait_ge(b.ve, b.cnt["ve"])
        b.pump(1)

    for tc in range(4):
        @b.block.tensor
        def _(t, tc=tc, ve=b.cnt["ve"]):
            t.wait_ge(b.ve, ve)
            b.inc(t.matmul(out=ps_r[:], lhsT=r16[:],
                           rhs=delta_bf[:, tc * 512:(tc + 1) * 512],
                           start=True, stop=True), "pe")

        @b.block.vector
        def _(v, tc=tc, pe=b.cnt["pe"]):
            v.wait_ge(b.pe, pe)
            b.inc(v.tensor_copy(out=drep[:, tc * 512:(tc + 1) * 512], in_=ps_r[:]),
                  "ve")

    @b.block.gpsimd
    def _(g):
        for k in range(DT):
            src = AP(a_in.tensor, k * 128 * S, [[S, 128], [1, S]])
            b.inc(g.dma_start(out=acol[:, k * S:(k + 1) * S], in_=src,
                              allow_slow_non_contiguous=True), "qg", 16)

    @b.block.scalar
    def _(sc, qg=b.cnt["qg"]):
        sc.wait_ge(b.qg, qg)
        b.inc(sc.activation(out=acol[:], in_=acol[:], func=AF.Sigmoid), "ac")
        sc.wait_ge(b.ac, b.cnt["ac"])
        b.inc(sc.activation(out=lna[:], in_=acol[:], func=AF.Ln), "ac")
    lna_ms = b.cnt["ac"]

    for k in range(DT):
        pj = k % 2

        @b.block.vector
        def _(v, k=k, pj=pj, ac=b.cnt["ac"], lna_ms=lna_ms):
            v.wait_ge(b.ac, max(ac, lna_ms))
            lnab = lna.ap()[:, k * S:(k + 1) * S].unsqueeze(1).broadcast_to((128, 32, S))
            b.inc(v.tensor_tensor(out=ptmp[pj].ap().rearrange("p (l s) -> p l s", s=S),
                                  in0=eiota.ap().rearrange("p (l s) -> p l s", s=S),
                                  in1=lnab, op=ALU.mult), "ve")

        @b.block.scalar
        def _(sc, k=k, pj=pj, ve=b.cnt["ve"]):
            sc.wait_ge(b.ve, ve)
            b.inc(sc.activation(out=pstr[:, k * 512:(k + 1) * 512], in_=ptmp[pj][:],
                                func=AF.Exp), "ac")
        b.pump(1)
    pows_ms = ("ac", b.cnt["ac"])

    @b.block.gpsimd
    def _(g, cms=consts_ms):
        g.wait_ge(getattr(b, cms[0]), cms[1])
        for gi in range(4):
            b.inc(g.dma_start(out=d_zpad[gi, :, 0:31], in_=zhead[:]), "qg", 16)

    ps_y = [b.ps("sc_psy", [128, 512]) for _ in range(2)]
    z_ms = []
    n = 0
    for gi in range(4):
        for tc in range(4):
            pj = n % 2

            @b.block.tensor
            def _(t, gi=gi, tc=tc, pj=pj, n=n, pm=pows_ms):
                if n == 0:
                    t.wait_ge(getattr(b, pm[0]), pm[1])
                if n >= 2:
                    sname, v = z_ms[n - 2]
                    t.wait_ge(getattr(b, sname), v)
                last = None
                for kt in range(DT):
                    last = t.matmul(
                        out=ps_y[pj][:],
                        lhsT=pstr[:, kt * 512 + gi * 128: kt * 512 + (gi + 1) * 128],
                        rhs=xstr[kt][:, tc * 512:(tc + 1) * 512],
                        start=(kt == 0), stop=(kt == DT - 1))
                b.inc(last, "pe")

            @b.block.vector
            def _(v, gi=gi, tc=tc, pj=pj, n=n, pe=b.cnt["pe"]):
                v.wait_ge(b.pe, pe)
                if n >= 2:
                    v.wait_ge(b.qg, b.cnt["qg"] - 16)
                b.inc(v.tensor_mul(out=zt[n % 2][:], in0=ps_y[pj][:],
                                   in1=drep[:, tc * 512:(tc + 1) * 512]), "ve")
            zv = b.cnt["ve"]

            @b.block.gpsimd
            def _(g, gi=gi, tc=tc, n=n, zv=zv):
                g.wait_ge(b.ve, zv)
                b.inc(g.dma_start(out=d_zpad[gi, :, 31 + tc * 512: 31 + (tc + 1) * 512],
                                  in_=zt[n % 2][:]), "qg", 16)
            z_ms.append(("ve", zv))
            b.pump(1)
            n += 1
    zstores = ("qg", b.cnt["qg"])

    for k in range(DT):
        @b.block.gpsimd
        def _(g, k=k, pe=b.cnt["pe"]):
            if k >= 1:
                g.wait_ge(b.pe, pe)
            src = AP(cw_in.tensor, k * 128 * S, [[S, 128], [1, S]])
            b.inc(g.dma_start(out=cw_fm[:], in_=src,
                              allow_slow_non_contiguous=True), "qg", 16)

        @b.block.tensor
        def _(t, k=k, qg=b.cnt["qg"], ve=b.cnt["ve"]):
            t.wait_ge(b.qg, qg)
            if k >= 1:
                t.wait_ge(b.ve, ve)
            b.inc(t.transpose(ps_tb[:], cw_fm[:], ident[:]), "pe")

        @b.block.vector
        def _(v, k=k, pe=b.cnt["pe"]):
            v.wait_ge(b.pe, pe)
            b.inc(v.tensor_copy(out=cwT[:, k * 128:(k + 1) * 128], in_=ps_tb[:]), "ve")

    for tc in range(4):
        @b.block.tensor
        def _(t, tc=tc, ve=b.cnt["ve"]):
            t.wait_ge(b.ve, ve)
            b.inc(t.matmul(out=ps_r[:], lhsT=r16[:],
                           rhs=cwT[:, tc * 512:(tc + 1) * 512],
                           start=True, stop=True), "pe")

        @b.block.vector
        def _(v, tc=tc, pe=b.cnt["pe"]):
            v.wait_ge(b.pe, pe)
            b.inc(v.tensor_copy(out=cwrep[:, tc * 512:(tc + 1) * 512], in_=ps_r[:]),
                  "ve")

    @b.block.sync
    def _(q, zs=zstores):
        q.wait_ge(getattr(b, zs[0]), zs[1])
        for gi in range(4):
            src = AP(d_zpad.tensor, gi * 128 * ZROW + (24 - 8 * gi),
                     [[16 * ZROW + 1, 8], [ZROW, 16], [1, T]])
            b.inc(q.dma_start(out=zsh[gi][:], in_=src), "qs", 16)
    zsh_ld = ("qs", b.cnt["qs"])

    def evac(mb, tc, psum):
        n2 = mb * 4 + tc
        oj = n2 % 2

        @b.block.vector
        def _(v, mb=mb, tc=tc, oj=oj, psum=psum, pe=b.cnt["pe"], n2=n2):
            v.wait_ge(b.pe, pe)
            if n2 >= 2:
                v.wait_ge(b.qg, b.cnt["qg"] - 16)
            b.inc(v.scalar_tensor_tensor(out=ys[oj][:],
                                         in0=xstr[mb][:, tc * 512:(tc + 1) * 512],
                                         scalar=dp_cols[:, mb:mb + 1], in1=psum[:],
                                         op0=ALU.mult, op1=ALU.add), "ve")
        ms = ("ve", b.cnt["ve"])

        @b.block.gpsimd
        def _(g, mb=mb, tc=tc, oj=oj, ve=b.cnt["ve"]):
            g.wait_ge(b.ve, ve)
            b.inc(g.dma_start(out=y_t[mb][:, tc * 512:(tc + 1) * 512], in_=ys[oj][:]),
                  "qg", 16)
        return ms

    mm_grid(b, DT, 4, 4,
            lambda gi, mb: cwrep[:, mb * 128:(mb + 1) * 128],
            lambda gi, tc: zsh[gi][:, tc * 512:(tc + 1) * 512],
            evac, pre_wait=[zsh_ld], pump_per=1)
    b.pop()



def emit_ffn(b, tag, d_wgT, d_wuT, d_wdT, nb, d_xn, d_s, d_u, down_wait=None):
    for tp in range(2):
        b.push()
        wg = [b.sb(f"{tag}_wg", [128, D], BF16) for _ in range(2)]
        wu = [b.sb(f"{tag}_wu", [128, D], BF16) for _ in range(2)]
        xnf = [b.sb(f"{tag}_xn", [128, 1024], BF16) for _ in range(DT)]
        hsb = b.sb(f"{tag}_h", [128, nb * 1024], BF16)
        sg_t = b.sb(f"{tag}_sg", [128, 1024], BF16)
        ps_g = [b.ps(f"{tag}_psg", [128, 512]) for _ in range(2)]
        ps_u = [b.ps(f"{tag}_psu", [128, 512]) for _ in range(2)]
        s_rows = d_s.rearrange("(n p) c -> n p c", p=128)
        u_rows = d_u.rearrange("(n p) c -> n p c", p=128)

        @b.block.sync
        def _(q):
            xt = d_xn.rearrange("(a p) c -> a p c", p=128)
            for k in range(DT):
                b.inc(q.dma_start(out=xnf[k][:],
                                  in_=xt[k][:, tp * 1024:(tp + 1) * 1024]), "qs", 16)

        qs0 = b.cnt["qs"]
        pe0 = b.cnt["pe"]
        ac0 = b.cnt["ac"]
        ve0 = b.cnt["ve"]

        @b.block.sync
        def _(q):
            for ib in range(nb):
                if ib >= 2:
                    q.wait_ge(b.ve, ve0 + (ib - 1) * 2)
                b.inc(q.dma_start(out=wg[ib % 2][:],
                                  in_=d_wgT[ib].rearrange("p k m -> p (k m)")), "qs", 16)
                b.inc(q.dma_start(out=wu[ib % 2][:],
                                  in_=d_wuT[ib].rearrange("p k m -> p (k m)")), "qs", 16)

        @b.block.tensor
        def _(t):
            for ib in range(nb):
                t.wait_ge(b.qs, qs0 + (ib + 1) * 32)
                if ib >= 1:
                    t.wait_ge(b.ve, ve0 + ib * 2)
                for w, psl in ((wg[ib % 2], ps_g), (wu[ib % 2], ps_u)):
                    for nck in range(2):
                        last = None
                        for kt in range(DT):
                            last = t.matmul(
                                out=psl[nck][:], lhsT=w[:, kt * 128:(kt + 1) * 128],
                                rhs=xnf[kt][:, nck * 512:(nck + 1) * 512],
                                start=(kt == 0), stop=(kt == DT - 1))
                    b.inc(last, "pe")

        @b.block.scalar
        def _(sc):
            for ib in range(nb):
                sc.wait_ge(b.pe, pe0 + 2 * ib + 1)
                b.inc(sc.activation(out=sg_t[:, 0:512], in_=ps_g[0][:], func=AF.Silu),
                      "ac")
                b.inc(sc.activation(out=sg_t[:, 512:1024], in_=ps_g[1][:],
                                    func=AF.Silu), "ac")

        @b.block.vector
        def _(v):
            for ib in range(nb):
                v.wait_ge(b.ac, ac0 + 2 * ib + 2)
                v.wait_ge(b.pe, pe0 + 2 * ib + 2)
                b.inc(v.tensor_mul(out=hsb[:, ib * 1024: ib * 1024 + 512],
                                   in0=sg_t[:, 0:512], in1=ps_u[0][:]), "ve")
                b.inc(v.tensor_mul(out=hsb[:, ib * 1024 + 512: (ib + 1) * 1024],
                                   in0=sg_t[:, 512:1024], in1=ps_u[1][:]), "ve")
        assert b.cnt["pe"] == pe0 + 2 * nb and b.cnt["ve"] == ve0 + 2 * nb

        wd = [b.sb(f"{tag}_wd", [128, nb * 128], BF16) for _ in range(2)]
        sst = [b.sb(f"{tag}_ss", [128, 1024], BF16) for _ in range(2)]
        g_t = b.sb(f"{tag}_g", [128, 1024], BF16)
        u_t = [b.sb(f"{tag}_u", [128, 1024], BF16) for _ in range(2)]
        ps_dn = [b.ps(f"{tag}_psd", [128, 512]) for _ in range(2)]
        qsD = b.cnt["qs"]
        peD = b.cnt["pe"]
        acD = b.cnt["ac"]
        veD = b.cnt["ve"]
        qgD = b.cnt["qg"]

        @b.block.sync
        def _(q):
            for eb in range(DT):
                if eb == 0 and down_wait is not None:
                    q.wait_ge(getattr(b, down_wait[0]), down_wait[1])
                if eb >= 2:
                    q.wait_ge(b.ve, veD + eb - 1)
                b.inc(q.dma_start(out=wd[eb % 2][:],
                                  in_=d_wdT[eb].rearrange("p k m -> p (k m)")), "qs", 16)
                b.inc(q.dma_start(out=sst[eb % 2][:],
                                  in_=s_rows[eb][:, tp * 1024:(tp + 1) * 1024]), "qs", 16)

        @b.block.tensor
        def _(t):
            for eb in range(DT):
                if eb == 0 and down_wait is not None:
                    t.wait_ge(getattr(b, down_wait[0]), down_wait[1])
                t.wait_ge(b.qs, qsD + (eb + 1) * 32)
                if eb >= 1:
                    t.wait_ge(b.ve, veD + eb)
                for nck in range(2):
                    last = None
                    for kb in range(nb):
                        last = t.matmul(
                            out=ps_dn[nck][:], lhsT=wd[eb % 2][:, kb * 128:(kb + 1) * 128],
                            rhs=hsb[:, kb * 1024 + nck * 512: kb * 1024 + (nck + 1) * 512],
                            start=(kb == 0), stop=(kb == nb - 1))
                    b.inc(last, "pe")

        @b.block.scalar
        def _(sc):
            for eb in range(DT):
                sc.wait_ge(b.pe, peD + 2 * eb + 2)
                b.inc(sc.activation(out=g_t[:, 0:512], in_=ps_dn[0][:], func=AF.Silu),
                      "ac")
                b.inc(sc.activation(out=g_t[:, 512:1024], in_=ps_dn[1][:],
                                    func=AF.Silu), "ac")

        @b.block.vector
        def _(v):
            for eb in range(DT):
                v.wait_ge(b.ac, acD + 2 * eb + 2)
                v.wait_ge(b.qs, qsD + (eb + 1) * 32)
                if eb >= 2:
                    v.wait_ge(b.qg, qgD + (eb - 1) * 16)
                b.inc(v.tensor_mul(out=u_t[eb % 2][:], in0=g_t[:], in1=sst[eb % 2][:]),
                      "ve")

        @b.block.gpsimd
        def _(g):
            for eb in range(DT):
                g.wait_ge(b.ve, veD + eb + 1)
                b.inc(g.dma_start(out=u_rows[eb][:, tp * 1024:(tp + 1) * 1024],
                                  in_=u_t[eb % 2][:]), "qg", 16)
        assert b.cnt["pe"] == peD + 2 * DT and b.cnt["ve"] == veD + DT

        b.pop()



def emit_cos_table(b, cos_t, tag):
    b.push()
    fcol = b.sb(f"{tag}_f", [128, 1], F32)
    iot = b.sb(f"{tag}_io", [128, T], F32)
    uu = b.sb(f"{tag}_uu", [128, T], F32)
    ki = b.sb(f"{tag}_ki", [128, T], I32)
    kf = b.sb(f"{tag}_kf", [128, T], F32)

    @b.block.gpsimd
    def _(g):
        g.iota(iot[:], pattern=[[1, T]], base=0, channel_multiplier=0,
               allow_small_or_imprecise_dtypes=True)
        i = g.iota(fcol[:], pattern=[[0, 1]], base=0, channel_multiplier=1,
                   allow_small_or_imprecise_dtypes=True)
        b.inc(i, "po")

    hicol = b.sb(f"{tag}_hi", [128, 1], F32)

    @b.block.vector
    def _(v, po=b.cnt["po"]):
        v.wait_ge(b.po, po)
        b.inc(v.tensor_scalar(out=hicol[:], in0=fcol[:], scalar1=64.0,
                              scalar2=None, op0=ALU.is_ge), "ve")
        v.wait_ge(b.ve, b.cnt["ve"])
        b.inc(v.scalar_tensor_tensor(out=fcol[:], in0=hicol[:], scalar=-64.0,
                                     in1=fcol[:], op0=ALU.mult, op1=ALU.add), "ve")
        v.wait_ge(b.ve, b.cnt["ve"])
        b.inc(v.tensor_scalar_mul(out=fcol[:], in0=fcol[:],
                                  scalar1=float(-2.0 * LN1E4 / HD)), "ve")
        v.wait_ge(b.ve, b.cnt["ve"])

    @b.block.scalar
    def _(s, ve=b.cnt["ve"]):
        s.wait_ge(b.ve, ve)
        b.inc(s.activation(out=fcol[:], in_=fcol[:], func=AF.Exp), "ac")

    @b.block.vector
    def _(v, ac=b.cnt["ac"]):
        v.wait_ge(b.ac, ac)
        b.inc(v.tensor_scalar(out=uu[:], in0=iot[:], scalar1=fcol[:],
                              scalar2=float(1.0 / (2 * PI)),
                              op0=ALU.mult, op1=ALU.mult), "ve")
        v.wait_ge(b.ve, b.cnt["ve"])
        b.inc(v.tensor_copy(out=ki[:], in_=uu[:]), "ve")
        v.wait_ge(b.ve, b.cnt["ve"])
        b.inc(v.tensor_copy(out=kf[:], in_=ki[:]), "ve")
        v.wait_ge(b.ve, b.cnt["ve"])
        b.inc(v.tensor_sub(out=uu[:], in0=uu[:], in1=kf[:]), "ve")
        v.wait_ge(b.ve, b.cnt["ve"])

    @b.block.scalar
    def _(s, ve=b.cnt["ve"]):
        s.wait_ge(b.ve, ve)
        b.inc(s.activation(out=kf[:], in_=uu[:], func=AF.Sin, scale=float(PI)), "ac")
        s.wait_ge(b.ac, b.cnt["ac"])
        b.inc(s.activation(out=kf[:], in_=kf[:], func=AF.Square), "ac")

    @b.block.vector
    def _(v, ac=b.cnt["ac"]):
        v.wait_ge(b.ac, ac)
        b.inc(v.tensor_scalar(out=cos_t[:], in0=kf[:], scalar1=-2.0, scalar2=1.0,
                              op0=ALU.mult, op1=ALU.add), "ve")
    b.pop()


def emit_rope(b, dst128, cos_t, rp1, sgncol, wait):
    @b.block.vector
    def _(v, wait=wait):
        for sname, val in wait:
            v.wait_ge(getattr(b, sname), val)
        for c0 in (0, 1024):
            b.inc(v.tensor_copy(out=rp1[0:64, :], in_=dst128[64:128, c0:c0 + 1024]),
                  "ve")
            b.inc(v.tensor_copy(out=rp1[64:128, :], in_=dst128[0:64, c0:c0 + 1024]),
                  "ve")
            v.wait_ge(b.ve, b.cnt["ve"])
            b.inc(v.scalar_tensor_tensor(out=rp1[:], in0=rp1[:], scalar=sgncol[:],
                                         in1=dst128[:, c0:c0 + 1024],
                                         op0=ALU.mult, op1=ALU.add), "ve")
            v.wait_ge(b.ve, b.cnt["ve"])
            b.inc(v.tensor_mul(out=dst128[:, c0:c0 + 1024], in0=rp1[:],
                               in1=cos_t[:, c0:c0 + 1024]), "ve")
            v.wait_ge(b.ve, b.cnt["ve"])


def emit_l1_attn(b, d_xn1, wq_in, wk_in, wv_in, wo_in, onesb, onesr, d_q, d_s1,
                 l1sel=None):
    b.push()
    k_fm = b.sb("at_kfm", [128, NKV * T], BF16)
    v_tm = b.sb("at_vtm", [128, 16 * 512], BF16)
    qt = [b.sb("at_q", [128, T], BF16) for _ in range(2)]
    q_rows = d_q.rearrange("(n p) c -> n p c", p=128)
    s_rows = d_s1.rearrange("(n p) c -> n p c", p=128)

    b.push()
    xstr = [b.sb("at_x", [128, T], BF16) for _ in range(DT)]
    cos_t = b.sb("at_cos", [128, T], F32)
    cos_k = b.sb("at_cosk", [128, T], F32)
    rp1 = b.sb("at_rp1", [128, 1024], F32)
    sgncol = b.sb("at_sgn", [128, 1], F32)

    @b.block.vector
    def _(v):
        b.inc(v.memset(sgncol[0:64, :], -1.0), "ve")
        b.inc(v.memset(sgncol[64:128, :], 1.0), "ve")
    emit_cos_table(b, cos_t, "ct")

    @b.block.vector
    def _(v, ve=b.cnt["ve"]):
        v.wait_ge(b.ve, ve)
        b.inc(v.tensor_scalar_mul(out=cos_k[:], in0=cos_t[:],
                                  scalar1=float(HD ** -0.5)), "ve")
        v.wait_ge(b.ve, b.cnt["ve"])
    load_stripes(b, xstr, d_xn1)
    xn_ld = ("qs", b.cnt["qs"])

    b.push()
    wstr = [b.sb("at_w", [128, D], BF16) for _ in range(DT)]
    cast_load_stripes(b, wstr, wq_in)
    w_ld = ("qg", b.cnt["qg"])
    q_free = {}

    def evac_q(mb, tc, psum):
        j = mb % 2

        @b.block.vector
        def _(v_, mb=mb, tc=tc, j=j, psum=psum, pe=b.cnt["pe"]):
            v_.wait_ge(b.pe, pe)
            if tc == 0 and mb >= 2:
                sname, v = q_free[mb - 2]
                v_.wait_ge(getattr(b, sname), v)
            b.inc(v_.tensor_copy(out=qt[j][:, tc * 512:(tc + 1) * 512], in_=psum[:]),
                  "ve")
        ms = ("ve", b.cnt["ve"])
        if tc == 3:
            emit_rope(b, qt[j].ap(), cos_t.ap(), rp1, sgncol, [ms])

            @b.block.gpsimd
            def _(g, mb=mb, j=j, ve=b.cnt["ve"]):
                g.wait_ge(b.ve, ve)
                b.inc(g.dma_start(out=q_rows[mb], in_=qt[j][:]), "qg", 16)
            q_free[mb] = ("qg", b.cnt["qg"])
            return ("ve", b.cnt["ve"])
        return ms

    mm_grid(b, NH, 4, DT,
            lambda kt, mb: wstr[kt][:, mb * 128:(mb + 1) * 128],
            lambda kt, tc: xstr[kt][:, tc * 512:(tc + 1) * 512],
            evac_q, pre_wait=[xn_ld, w_ld])
    b.pop()

    b.push()
    kv_w = [b.sb("at_kw", [128, 512], BF16) for _ in range(DT)]
    cast_load_stripes(b, kv_w, wk_in)
    kw_ld = ("qg", b.cnt["qg"])

    def evac_k(mb, tc, psum):
        @b.block.vector
        def _(v_, mb=mb, tc=tc, psum=psum, pe=b.cnt["pe"]):
            v_.wait_ge(b.pe, pe)
            b.inc(v_.tensor_copy(out=k_fm[:, mb * T + tc * 512: mb * T + (tc + 1) * 512],
                                 in_=psum[:]), "ve")
        ms = ("ve", b.cnt["ve"])
        if tc == 3:
            emit_rope(b, k_fm.ap()[:, mb * T:(mb + 1) * T], cos_k.ap(), rp1, sgncol, [ms])
            return ("ve", b.cnt["ve"])
        return ms

    mm_grid(b, NKV, 4, DT,
            lambda kt, mb: kv_w[kt][:, mb * 128:(mb + 1) * 128],
            lambda kt, tc: xstr[kt][:, tc * 512:(tc + 1) * 512],
            evac_k, pre_wait=[kw_ld])
    cast_load_stripes(b, kv_w, wv_in)
    vw_ld = ("qg", b.cnt["qg"])

    def evac_v(mb, tc, psum):
        @b.block.vector
        def _(v_, mb=mb, psum=psum, pe=b.cnt["pe"]):
            v_.wait_ge(b.pe, pe)
            b.inc(v_.tensor_copy(out=v_tm[:, mb * 512:(mb + 1) * 512], in_=psum[:]),
                  "ve")
        return ("ve", b.cnt["ve"])

    mm_grid(b, 16, 1, DT,
            lambda kt, mb: xstr[kt][:, mb * 128:(mb + 1) * 128],
            lambda kt, tc: kv_w[kt][:, 0:512],
            evac_v, pre_wait=[vw_ld])
    b.pop()
    b.pop()

    b.push()
    att = b.sb("at_att", [128, NH * T], BF16)
    b.push()
    if l1sel is not None:
        queue_selects(b, "p1", l1sel[0], scol_map=l1sel[1])
    exps = [b.sb("at_exps", [128, 1024], BF16) for _ in range(2)]
    avt = b.sb("at_avt", [128, 1024], BF16)
    rc_row = b.sb("at_rc", [1, 1024], F32)
    ps_s = [b.ps("at_pss", [128, 1024]) for _ in range(2)]
    ps_av = b.ps("at_psav", [128, 1024])
    ps_n = b.ps("at_psn", [1, 1024])
    att_done = []
    b.att_last_ve = b.cnt["ve"]

    for h in range(NH):
        kv = h // 4
        j = h % 2

        @b.block.sync
        def _(q, h=h, j=j):
            if h >= 2:
                sname, v = att_done[h - 2]
                q.wait_ge(getattr(b, sname), v)
            b.inc(q.dma_start(out=qt[j][:], in_=q_rows[h]), "qs", 16)
        q_ld = b.cnt["qs"]

        for qh in range(2):
            prev_ve = b.att_last_ve
            pe0 = b.cnt["pe"]
            ac0 = b.cnt["ac"]

            def score_mm(t, kb, pj):
                last = None
                for ncq in range(2):
                    last = t.matmul(
                        out=ps_s[pj][:, ncq * 512:(ncq + 1) * 512],
                        lhsT=k_fm[:, kv * T + kb * 128: kv * T + (kb + 1) * 128],
                        rhs=qt[j][:, qh * 1024 + ncq * 512: qh * 1024 + (ncq + 1) * 512],
                        start=True, stop=True)
                b.inc(last, "pe")

            def av_mm(t, kb, pj):
                last = None
                for ncq in range(2):
                    t.matmul(out=ps_av[:, ncq * 512:(ncq + 1) * 512],
                             lhsT=v_tm[:, kb * 512 + kv * 128: kb * 512 + (kv + 1) * 128],
                             rhs=exps[pj][:, ncq * 512:(ncq + 1) * 512],
                             start=(kb == 0), stop=(kb == 15))
                    last = t.matmul(out=ps_n[:, ncq * 512:(ncq + 1) * 512],
                                    lhsT=onesb[:],
                                    rhs=exps[pj][:, ncq * 512:(ncq + 1) * 512],
                                    start=(kb == 0), stop=(kb == 15))
                b.inc(last, "pe")

            @b.block.tensor
            def _(t, q_ld=q_ld, prev_ve=prev_ve, pe0=pe0, ac0=ac0, qh=qh):
                for kb in range(16):
                    if kb == 0 and qh == 0:
                        t.wait_ge(b.qs, q_ld)
                    if kb <= 1:
                        t.wait_ge(b.ve, prev_ve)
                    if kb >= 2:
                        t.wait_ge(b.ac, ac0 + kb - 1)
                    score_mm(t, kb, kb % 2)
                    if kb >= 1:
                        t.wait_ge(b.ac, ac0 + kb)
                        av_mm(t, kb - 1, (kb - 1) % 2)
                t.wait_ge(b.ac, ac0 + 16)
                av_mm(t, 15, 1)

            @b.block.scalar
            def _(sc, pe0=pe0, ac0=ac0):
                for kb in range(16):
                    sc.wait_ge(b.pe, pe0 + (kb + 1 if kb <= 1 else 2 * kb))
                    b.inc(sc.activation(out=exps[kb % 2][:], in_=ps_s[kb % 2][:],
                                        func=AF.Exp), "ac")
            assert b.cnt["pe"] == pe0 + 32 and b.cnt["ac"] == ac0 + 16

            @b.block.vector
            def _(v, pe=b.cnt["pe"]):
                v.wait_ge(b.pe, pe)
                b.inc(v.reciprocal(out=rc_row[:], in_=ps_n[:]), "ve")
                v.wait_ge(b.ve, b.cnt["ve"])
                b.inc(v.tensor_copy(out=avt[:], in_=ps_av[:]), "ve")

            @b.block.tensor
            def _(t, ve=b.cnt["ve"]):
                t.wait_ge(b.ve, ve)
                last = None
                for ncq in range(2):
                    last = t.matmul(out=ps_s[1][:, ncq * 512:(ncq + 1) * 512],
                                    lhsT=onesr[:],
                                    rhs=rc_row[:, ncq * 512:(ncq + 1) * 512],
                                    start=True, stop=True)
                b.inc(last, "pe")

            @b.block.vector
            def _(v, h=h, qh=qh, pe=b.cnt["pe"]):
                v.wait_ge(b.pe, pe)
                b.inc(v.tensor_mul(
                    out=att[:, h * T + qh * 1024: h * T + (qh + 1) * 1024],
                    in0=avt[:], in1=ps_s[1][:]), "ve")
            b.att_last_ve = b.cnt["ve"]
        att_done.append(("ve", b.cnt["ve"]))
        b.pump(7)

    b.pump_drain()
    b.pop()
    att_ms = ("ve", b.cnt["ve"])

    wstr2 = [b.sb("at_w2", [128, D], BF16) for _ in range(DT)]
    cast_load_stripes(b, wstr2, wo_in)
    wo_ld = ("qg", b.cnt["qg"])
    os_ = [b.sb("at_o", [128, 512], BF16) for _ in range(2)]

    def evac_o(mb, tc, psum):
        oj = (mb * 4 + tc) % 2

        @b.block.vector
        def _(v_, mb=mb, tc=tc, oj=oj, psum=psum, pe=b.cnt["pe"], n=mb * 4 + tc):
            v_.wait_ge(b.pe, pe)
            if n >= 2:
                v_.wait_ge(b.qg, b.cnt["qg"] - 16)
            b.inc(v_.tensor_copy(out=os_[oj][:], in_=psum[:]), "ve")

        @b.block.gpsimd
        def _(g, mb=mb, tc=tc, oj=oj, ve=b.cnt["ve"]):
            g.wait_ge(b.ve, ve)
            b.inc(g.dma_start(out=s_rows[mb][:, tc * 512:(tc + 1) * 512],
                              in_=os_[oj][:]), "qg", 16)
        return ("ve", b.cnt["ve"])

    mm_grid(b, DT, 4, DT,
            lambda kt, mb: wstr2[kt][:, mb * 128:(mb + 1) * 128],
            lambda kt, tc: att[:, kt * T + tc * 512: kt * T + (tc + 1) * 512],
            evac_o, pre_wait=[wo_ld, att_ms])
    b.pop()
    b.pop()



def build_nc(upto=99, dbg=()):
    b = Bld()
    nc = b.nc
    dbg = set(dbg)

    x_in = b.dr("x", [T, D], F32, kind="ExternalInput")
    win = b.dr("proj_in_w", [D, D], F32, kind="ExternalInput")
    bin_v = b.dr("proj_in_b", [D], F32, kind="ExternalInput")
    wout = b.dr("proj_out_w", [D, D], F32, kind="ExternalInput")
    bout_v = b.dr("proj_out_b", [D], F32, kind="ExternalInput")
    a_in = b.dr("A", [D, S], F32, kind="ExternalInput")
    bw_in = b.dr("B_w", [S, D], F32, kind="ExternalInput")
    cw_in = b.dr("C_w", [D, S], F32, kind="ExternalInput")
    dp_v = b.dr("D_param", [D], F32, kind="ExternalInput")
    lut_in = b.dr("lut", [16], F32, kind="ExternalInput")
    gidx = b.dr("gate_idx", [I0, D], I32, kind="ExternalInput")
    uidx = b.dr("up_idx", [I0, D], I32, kind="ExternalInput")
    didx = b.dr("down_idx", [D, I0], I32, kind="ExternalInput")
    wq_in = b.dr("Wq", [D, D], F32, kind="ExternalInput")
    wk_in = b.dr("Wk", [D, 512], F32, kind="ExternalInput")
    wv_in = b.dr("Wv", [D, 512], F32, kind="ExternalInput")
    wo_in = b.dr("Wo", [D, D], F32, kind="ExternalInput")
    wqg = b.dr("wq_gate", [I1, D], I32, kind="ExternalInput")
    sg_v = b.dr("s_gate", [I1], F32, kind="ExternalInput")
    wqu = b.dr("wq_up", [I1, D], I32, kind="ExternalInput")
    su_v = b.dr("s_up", [I1], F32, kind="ExternalInput")
    wqd = b.dr("wq_down", [D, I1], I32, kind="ExternalInput")
    sd_v = b.dr("s_down", [D], F32, kind="ExternalInput")
    out = b.dr("out", [T, D], F32, kind="ExternalOutput")

    d_xn0 = b.dr("d_xn0", [T, D], BF16)
    d_xnf0 = b.dr("d_xnf0", [D, T], BF16)
    d_xnf1 = b.dr("d_xnf1", [D, T], BF16)
    d_xp = b.dr("d_xp", [D, T], BF16)
    d_y0 = b.dr("d_y0", [D, T], BF16)
    d_s0 = b.dr("d_s0", [D, T], BF16)
    d_u0 = b.dr("d_u0", [D, T], BF16)
    d_x1 = b.dr("d_x1", [T, D], F32)
    d_xn1 = b.dr("d_xn1", [T, D], BF16)
    d_q = b.dr("d_q", [D, T], BF16)
    d_s1 = b.dr("d_s1", [D, T], BF16)
    d_u1 = b.dr("d_u1", [D, T], BF16)
    d_zpad = b.dr("d_zpad", [4, 128, ZROW], BF16)

    sb_bin = b.sb("c_bin", [128, DT], F32)
    sb_bout = b.sb("c_bout", [128, DT], F32)
    sb_dp = b.sb("c_dp", [128, DT], F32)
    sb_sg = b.sb("c_sg", [128, NB1], F32)
    sb_sgm = b.sb("c_sgm", [128, NB1], F32)
    sb_su = b.sb("c_su", [128, NB1], F32)
    sb_sum = b.sb("c_sum", [128, NB1], F32)
    sb_sd = b.sb("c_sd", [128, DT], F32)
    sb_sdm = b.sb("c_sdm", [128, DT], F32)
    sb_lutrow = b.sb("c_lutrow", [1, 16], F32)
    sb_lut128 = b.sb("c_lut128", [128, 16], F32)
    sb_onesb = b.sb("c_onesb", [128, 1], BF16)
    sb_onesr = b.sb("c_onesr", [1, 128], F32)
    sb_ident = b.sb("c_ident", [128, 128], BF16)

    block = b.es.enter_context(nc.Block())
    b.block = block

    load_vec_cols(b, sb_bin, bin_v, DT)
    load_vec_cols(b, sb_bout, bout_v, DT)
    load_vec_cols(b, sb_dp, dp_v, DT)
    load_vec_cols(b, sb_sg, sg_v, NB1)
    load_vec_cols(b, sb_su, su_v, NB1)
    load_vec_cols(b, sb_sd, sd_v, DT)

    @block.sync
    def _(q):
        b.inc(q.dma_start(out=sb_lutrow[:],
                          in_=AP(lut_in.tensor, 0, [[16, 1], [1, 16]])), "qs", 16)

    @block.vector
    def _(v, qs=b.cnt["qs"]):
        v.wait_ge(b.qs, qs)
        b.inc(v.memset(sb_onesb[:], 1.0), "ve")
        b.inc(v.memset(sb_onesr[:], 1.0), "ve")
        b.inc(v.tensor_scalar_mul(out=sb_sgm[:], in0=sb_sg[:], scalar1=-128.0), "ve")
        b.inc(v.tensor_scalar_mul(out=sb_sum[:], in0=sb_su[:], scalar1=-128.0), "ve")
        b.inc(v.tensor_scalar_mul(out=sb_sdm[:], in0=sb_sd[:], scalar1=-128.0), "ve")

    b.push()
    ps_l = b.ps("c_psl", [128, 16])

    @block.tensor
    def _(t, qs=b.cnt["qs"], ve=b.cnt["ve"]):
        t.wait_ge(b.qs, qs)
        t.wait_ge(b.ve, ve)
        b.inc(t.matmul(out=ps_l[:], lhsT=sb_onesr[:], rhs=sb_lutrow[:],
                       start=True, stop=True), "pe")

    @block.vector
    def _(v, pe=b.cnt["pe"]):
        v.wait_ge(b.pe, pe)
        b.inc(v.tensor_copy(out=sb_lut128[:], in_=ps_l[:]), "ve")

    idcolC = b.sb("c_idcol", [128, 128], F32)
    pcolC = b.sb("c_pcol", [128, 1], F32)

    @block.gpsimd
    def _(g):
        g.iota(idcolC[:], pattern=[[1, 128]], base=0, channel_multiplier=0,
               allow_small_or_imprecise_dtypes=True)
        i = g.iota(pcolC[:], pattern=[[0, 1]], base=0, channel_multiplier=1,
                   allow_small_or_imprecise_dtypes=True)
        b.inc(i, "po")

    @block.vector
    def _(v, po=b.cnt["po"]):
        v.wait_ge(b.po, po)
        b.inc(v.tensor_scalar(out=idcolC[:], in0=idcolC[:], scalar1=pcolC[:],
                              scalar2=None, op0=ALU.is_equal), "ve")
        v.wait_ge(b.ve, b.cnt["ve"])
        b.inc(v.tensor_copy(out=sb_ident[:], in_=idcolC[:]), "ve")
    b.pop()

    d_win_rows = b.dr("d_win_rows", [D, D], BF16)
    d_wout_rows = b.dr("d_wout_rows", [D, D], BF16)
    d_bw_rows = b.dr("d_bw_rows", [S, D], BF16)

    b.push()
    ca = [b.sb("cw_c", [128, D], BF16) for _ in range(2)]
    bwrow = b.sb("cw_bw", [16, D], BF16)

    sms = []

    def cast_rows(src, dst):
        st = src.rearrange("(n p) c -> n p c", p=128)
        dt_ = dst.rearrange("(n p) c -> n p c", p=128)
        for i in range(DT):
            @b.block.gpsimd
            def _(g, i=i, nglob=len(sms)):
                if nglob >= 2:
                    sname, v = sms[nglob - 2]
                    g.wait_ge(getattr(b, sname), v)
                b.inc(g.dma_start(out=ca[i % 2][:], in_=st[i]), "qg", 16)

            @b.block.sync
            def _(q, i=i, qg=b.cnt["qg"]):
                q.wait_ge(b.qg, qg)
                b.inc(q.dma_start(out=dt_[i], in_=ca[i % 2][:]), "qs", 16)
            sms.append(("qs", b.cnt["qs"]))

    cast_rows(win, d_win_rows)
    cast_rows(wout, d_wout_rows)

    @b.block.gpsimd
    def _(g):
        b.inc(g.dma_start(out=bwrow[:], in_=bw_in[:, :]), "qg", 16)

    @b.block.sync
    def _(q, qg=b.cnt["qg"]):
        q.wait_ge(b.qg, qg)
        b.inc(q.dma_start(out=d_bw_rows[:, :], in_=bwrow[:]), "qs", 16)
    b.pop()

    if upto >= 2:
        emit_resid_ln(b, "ln0", x_in, None, None, d_xn0)
        emit_fm_copy(b, "xnf0", d_xn0, T, D, d_xnf0)
    if upto >= 6:
        b.push()
        rows0 = queue_selects(b, "p0", [
            ("wg0", gidx, I0, D, "select"),
            ("wu0", uidx, I0, D, "select"),
            ("wd0", didx, D, I0, "select"),
        ], lut128=sb_lut128, ident=sb_ident)
    if upto >= 3:
        emit_proj_in(b, d_win_rows, d_xnf0, sb_bin, d_xp)
    if upto >= 4:
        emit_scan(b, bw_in, a_in, cw_in, sb_dp, sb_onesr, d_bw_rows, d_xp,
                  d_zpad, d_y0)
    if upto >= 5:
        emit_proj_out(b, d_wout_rows, d_y0, sb_bout, d_xnf0, d_s0)
    if upto >= 6:
        b.pump_drain()
        b.pop()
        tl0 = emit_tilings(b, "t0", [
            ("wg0", rows0["wg0"], I0, D),
            ("wu0", rows0["wu0"], I0, D),
        ])
        d_wgT0, d_wuT0 = tl0["wg0"], tl0["wu0"]
        tl0d = emit_tilings(b, "t0d", [("wd0", rows0["wd0"], D, I0)])
        d_wdT0 = tl0d["wd0"]
    if upto >= 7:
        emit_ffn(b, "f0", d_wgT0, d_wuT0, d_wdT0, NB0, d_xnf0, d_s0, d_u0,
                 down_wait=tl0d["_ms"] if upto >= 6 else None)
    if upto >= 8:
        emit_resid_ln(b, "fin0", x_in, d_u0, d_x1, d_xn1)
        emit_fm_copy(b, "xnf1", d_xn1, T, D, d_xnf1)
    if upto >= 9:
        l1sel = None
        if upto >= 10:
            l1sel = ([("wg1", wqg, I1, D, "dequant"),
                      ("wu1", wqu, I1, D, "dequant"),
                      ("wd1", wqd, D, I1, "dequant")],
                     {"wg1": (sb_sg, sb_sgm), "wu1": (sb_su, sb_sum),
                      "wd1": (sb_sd, sb_sdm)})
        emit_l1_attn(b, d_xnf1, wq_in, wk_in, wv_in, wo_in, sb_onesb, sb_onesr,
                     d_q, d_s1, l1sel=l1sel)
    if upto >= 10:
        tl1 = emit_tilings(b, "t1", [
            ("wg1", b.dram["d_wg1_rows"], I1, D),
            ("wu1", b.dram["d_wu1_rows"], I1, D),
        ])
        d_wgT1, d_wuT1 = tl1["wg1"], tl1["wu1"]
        tl1d = emit_tilings(b, "t1d", [("wd1", b.dram["d_wd1_rows"], D, I1)])
        d_wdT1 = tl1d["wd1"]
    if upto >= 11:
        emit_ffn(b, "f1", d_wgT1, d_wuT1, d_wdT1, NB1, d_xnf1, d_s1, d_u1,
                 down_wait=tl1d["_ms"] if upto >= 10 else None)
    if upto >= 12:
        emit_resid_ln(b, "fin1", d_x1, d_u1, out, None)

    dbg_specs = {
        "xn0": ("d_xn0", [T, D], BF16), "xp": ("d_xp", [D, T], BF16),
        "y0": ("d_y0", [D, T], BF16), "s0": ("d_s0", [D, T], BF16),
        "u0": ("d_u0", [D, T], BF16), "x1": ("d_x1", [T, D], F32),
        "xn1": ("d_xn1", [T, D], BF16), "q": ("d_q", [D, T], BF16),
        "s1": ("d_s1", [D, T], BF16), "u1": ("d_u1", [D, T], BF16),
        "wg0": ("d_wg0_rows", [I0, D], BF16),
        "wd0": ("d_wd0_rows", [D, I0], BF16),
        "wg1": ("d_wg1_rows", [I1, D], BF16),
        "zpad": ("d_zpad", [4, 128, ZROW], BF16),
    }
    emit = [(k, v) for k, v in dbg_specs.items() if k in dbg and v[0] in b.dram]
    outs = {}
    for k, (srcn, shape, dt) in emit:
        outs[k] = b.dr("o_" + k, shape, dt, kind="ExternalOutput")

    @block.sync
    def _(q):
        for name, v in b.cnt.items():
            if v > 0:
                q.wait_ge(getattr(b, name), v)
        for k, (srcn, shape, dt) in emit:
            b.inc(q.dma_start(out=outs[k], in_=b.dram[srcn]), "qs", 16)
        q.wait_ge(b.qs, b.cnt["qs"])

    b.barrier()
    return b



INPUT_MAP = {
    "proj_in_w": "L0_attn_proj_in_w", "proj_in_b": "L0_attn_proj_in_b",
    "proj_out_w": "L0_attn_proj_out_w", "proj_out_b": "L0_attn_proj_out_b",
    "A": "L0_attn_A", "B_w": "L0_attn_B_w", "C_w": "L0_attn_C_w",
    "D_param": "L0_attn_D_param", "lut": "L0_ffn_lut",
    "gate_idx": "L0_ffn_gate_idx", "up_idx": "L0_ffn_up_idx",
    "down_idx": "L0_ffn_down_idx",
    "Wq": "L1_attn_Wq", "Wk": "L1_attn_Wk", "Wv": "L1_attn_Wv",
    "Wo": "L1_attn_Wo",
    "wq_gate": "L1_ffn_wq_gate", "s_gate": "L1_ffn_s_gate",
    "wq_up": "L1_ffn_wq_up", "s_up": "L1_ffn_s_up",
    "wq_down": "L1_ffn_wq_down", "s_down": "L1_ffn_s_down",
}


def make_in_maps(inputs):
    shared = {k: np.ascontiguousarray(inputs[v]) for k, v in INPUT_MAP.items()}
    x = np.ascontiguousarray(inputs["L0_attn_x"])
    maps = []
    for c in range(8):
        m = dict(shared)
        m["x"] = np.ascontiguousarray(x[c])
        maps.append(m)
    return maps


def kernel(**inputs):
    b = build_nc()
    res = run_bass_kernel_spmd(b.nc, make_in_maps(inputs), core_ids=list(range(8)))
    return np.stack([np.asarray(r["out"], dtype=np.float32) for r in res.results])


# version 16
# speedup vs baseline: 1.0076x; 1.0076x over previous
import numpy as np
from contextlib import ExitStack

import concourse.bass as bass
import concourse.mybir as mybir
from concourse.ap import AP
from concourse.bass_utils import run_bass_kernel_spmd

F32 = mybir.dt.float32
BF16 = mybir.dt.bfloat16
I32 = mybir.dt.int32
AF = mybir.ActivationFunctionType
ALU = mybir.AluOpType
AXX = mybir.AxisListType.X

D = 2048
T = 2048
S = 16
I0, NB0 = 5504, 43
I1, NB1 = 5632, 44
NH, NKV, HD = 16, 4, 128
DT = D // 128
ZROW = 2112
PI = 3.14159265358979323846
LN1E4 = 9.210340371976184


class Bld:
    SEMS = ("qs", "qc", "qg", "pe", "ve", "ac", "po")

    def __init__(self):
        self.nc = bass.Bass("TRN2", target_bir_lowering=False)
        self.es = ExitStack()
        self.scopes = [self.es]
        self.cnt = {}
        for name in self.SEMS:
            h = self.es.enter_context(self.nc.semaphore(name))
            setattr(self, name, h)
            self.cnt[name] = 0
        self.dram = {}
        self._uid = 0
        self.pump_q = []
        self.pump_last = None

    def uid(self):
        self._uid += 1
        return self._uid

    def inc(self, inst, name, k=1):
        inst.then_inc(getattr(self, name), k)
        self.cnt[name] += k
        return self.cnt[name]

    def sb(self, name, shape, dt):
        return self.scopes[-1].enter_context(
            self.nc.sbuf_tensor(f"{name}_{self.uid()}", shape, dt))

    def ps(self, name, shape, dt=F32):
        return self.scopes[-1].enter_context(
            self.nc.psum_tensor(f"{name}_{self.uid()}", shape, dt))

    def dr(self, name, shape, dt, kind="Internal"):
        ap = self.nc.dram_tensor(name, shape, dt, kind=kind).ap()
        self.dram[name] = ap
        return ap

    def push(self):
        self.scopes.append(ExitStack())

    def pop(self):
        self.barrier()
        self.scopes.pop().close()

    def pump(self, n=1):
        for _ in range(n):
            if not self.pump_q:
                return
            fn = self.pump_q.pop(0)
            fn()

    def pump_drain(self):
        while self.pump_q:
            self.pump_q.pop(0)()

    def pump_fence(self):
        return self.pump_last

    def barrier(self):
        snap = [(name, v) for name, v in self.cnt.items() if v > 0]
        for eng in ("sync", "scalar", "vector", "tensor", "gpsimd"):
            def mk(e, snap=snap):
                for name, v in snap:
                    e.wait_ge(getattr(self, name), v)
            getattr(self.block, eng)(mk)



def load_vec_cols(b, dst, src_vec, n_chunks):
    src = AP(src_vec.tensor, 0, [[1, 128], [128, n_chunks]])

    @b.block.sync
    def _(q):
        b.inc(q.dma_start(out=dst[:, 0:n_chunks], in_=src,
                          allow_slow_non_contiguous=True), "qs", 16)


def mm_grid(b, nmb, ntc, nkt, lhsT_fn, rhs_fn, evac_fn, pre_wait=None, pump_per=0):
    b.push()
    psums = [b.ps("mmg", [128, 512]) for _ in range(2)]
    evac_ms = []
    n = 0
    for mb in range(nmb):
        for tc in range(ntc):
            pj = n % 2

            @b.block.tensor
            def _(t, n=n, pj=pj, mb=mb, tc=tc):
                if n == 0 and pre_wait is not None:
                    for name, v in pre_wait:
                        t.wait_ge(getattr(b, name), v)
                if n >= 2:
                    sname, v = evac_ms[n - 2]
                    t.wait_ge(getattr(b, sname), v)
                last = None
                for kt in range(nkt):
                    last = t.matmul(out=psums[pj][:], lhsT=lhsT_fn(kt, mb),
                                    rhs=rhs_fn(kt, tc),
                                    start=(kt == 0), stop=(kt == nkt - 1))
                b.inc(last, "pe")

            evac_ms.append(evac_fn(mb, tc, psums[pj]))
            b.pump(pump_per)
            n += 1
    b.pop()


def stream_stripes_T(b, dst_list, d_rows, queue="qs"):
    fence = b.pump_fence()

    @b.block.sync
    def _(q):
        if fence is not None:
            q.wait_ge(getattr(b, fence[0]), fence[1])
        for k, dst in enumerate(dst_list):
            b.inc(q.dma_start_transpose(dst[:], d_rows[:, k * 128:(k + 1) * 128]),
                  "qs", 16)


def load_stripes(b, dst_list, d_rows):
    rt = d_rows.rearrange("(n p) c -> n p c", p=128)

    @b.block.sync
    def _(q):
        for k, dst in enumerate(dst_list):
            b.inc(q.dma_start(out=dst[:], in_=rt[k]), "qs", 16)


def cast_load_stripes(b, dst_list, src_hbm):
    st = src_hbm.rearrange("(n p) c -> n p c", p=128)

    @b.block.gpsimd
    def _(g):
        for k, dst in enumerate(dst_list):
            b.inc(g.dma_start(out=dst[:], in_=st[k]), "qg", 16)



def _emit_select(b, eng, semname, src, dst, dst2, tmp, tmp2, lut128, mode,
                 scol_ap=None, smcol_ap=None):
    sem = getattr(b, semname)
    if mode == "select":
        b.inc(eng.tensor_scalar(out=dst, in0=src, scalar1=0.0,
                                scalar2=lut128[:, 0:1],
                                op0=ALU.is_equal, op1=ALU.mult), semname)
        b.inc(eng.tensor_scalar(out=dst2, in0=src, scalar1=1.0,
                                scalar2=lut128[:, 1:2],
                                op0=ALU.is_equal, op1=ALU.mult), semname)
        for vv in range(2, 16, 2):
            t0, t1 = (tmp, tmp2) if (vv // 2) % 2 == 0 else (tmp2, tmp)
            b.inc(eng.tensor_scalar(out=t0, in0=src, scalar1=float(vv),
                                    scalar2=lut128[:, vv:vv + 1],
                                    op0=ALU.is_equal, op1=ALU.mult), semname)
            b.inc(eng.tensor_scalar(out=t1, in0=src, scalar1=float(vv + 1),
                                    scalar2=lut128[:, vv + 1:vv + 2],
                                    op0=ALU.is_equal, op1=ALU.mult), semname)
            b.inc(eng.tensor_add(out=dst, in0=dst, in1=t0), semname)
            b.inc(eng.tensor_add(out=dst2, in0=dst2, in1=t1), semname)
        eng.wait_ge(sem, b.cnt[semname])
        b.inc(eng.tensor_add(out=dst, in0=dst, in1=dst2), semname)
        eng.wait_ge(sem, b.cnt[semname])
    elif mode == "dequant":
        b.inc(eng.tensor_scalar(out=dst, in0=src, scalar1=scol_ap,
                                scalar2=smcol_ap, op0=ALU.mult, op1=ALU.add),
              semname)
        eng.wait_ge(sem, b.cnt[semname])
    else:
        b.inc(eng.tensor_copy(out=dst, in_=src), semname)
        eng.wait_ge(sem, b.cnt[semname])


def prep_rows(b, name, src_hbm, R, C, mode, lut128=None, scol=None, smcol=None):
    nr = R // 128
    d_rows = b.dr(f"d_{name}_rows", [R, C], BF16)
    b.push()
    ia = [b.sb(f"{name}_i", [128, C], BF16) for _ in range(2)]
    wa = [b.sb(f"{name}_w", [128, C], BF16) for _ in range(2)]
    ta = [b.sb(f"{name}_t", [128, C], BF16) for _ in range(2)]
    wb2 = [b.sb(f"{name}_w2", [128, C], BF16) for _ in range(2)]
    tb2 = [b.sb(f"{name}_t2", [128, C], BF16) for _ in range(2)]
    src_t = src_hbm.rearrange("(n p) c -> n p c", p=128)
    dst_t = d_rows.rearrange("(n p) c -> n p c", p=128)
    v_store = []
    p_store = []

    use_pool = False
    vtiles = [i for i in range(nr) if not use_pool or i % 3 != 2]
    ptiles = [i for i in range(nr) if use_pool and i % 3 == 2]
    vpos = {i: k for k, i in enumerate(vtiles)}
    ppos = {i: k for k, i in enumerate(ptiles)}

    for i in range(nr):
        if i in vpos:
            k = vpos[i]
            j = k % 2

            @b.block.gpsimd
            def _(g, i=i, k=k, j=j):
                if k >= 2:
                    sname, v = v_store[k - 2]
                    g.wait_ge(getattr(b, sname), v)
                b.inc(g.dma_start(out=ia[j][:], in_=src_t[i]), "qg", 16)
            ld = ("qg", b.cnt["qg"])

            @b.block.vector
            def _(v, i=i, k=k, j=j, ld=ld):
                v.wait_ge(getattr(b, ld[0]), ld[1])
                if k >= 2:
                    sname, val = v_store[k - 2]
                    v.wait_ge(getattr(b, sname), val)
                sc = scol[:, i:i + 1] if scol is not None else None
                sm = smcol[:, i:i + 1] if smcol is not None else None
                _emit_select(b, v, "ve", ia[j][:], wa[j][:], wb2[j][:],
                             ta[j][:], tb2[j][:], lut128, mode, sc, sm)
            cm = ("ve", b.cnt["ve"])

            @b.block.sync
            def _(q, i=i, j=j, cm=cm):
                q.wait_ge(getattr(b, cm[0]), cm[1])
                b.inc(q.dma_start(out=dst_t[i], in_=wa[j][:]), "qs", 16)
            v_store.append(("qs", b.cnt["qs"]))
    b.pop()
    return d_rows


def queue_selects(b, tag, specs, lut128=None, scol_map=None, chalf=None,
                  ident=None):
    cmax = max(sp[3] for sp in specs)
    ia = [b.sb(f"{tag}_i", [128, cmax], BF16) for _ in range(2)]
    wa = [b.sb(f"{tag}_w", [128, cmax], BF16) for _ in range(2)]
    any_sel = any(sp[4] == "select" for sp in specs)
    if any_sel:
        mbuf = [b.sb(f"{tag}_m", [128, 1024], BF16) for _ in range(8)]
        ps_sel = b.ps(f"{tag}_ps", [128, 1024])
    sel_ms = []
    state = {"g": 0}
    rows = {}

    for name, src_hbm, R, C, mode in specs:
        d_rows = b.dr(f"d_{name}_rows", [R, C], BF16)
        rows[name] = d_rows
        scol = scol_map.get(name) if scol_map else None
        segs = [(c0, min(c0 + 1024, C)) for c0 in range(0, C, 1024)]
        for i in range(R // 128):
            def emit(i=i, name=name, src_hbm=src_hbm, R=R, C=C, mode=mode,
                     scol=scol, d_rows=d_rows, segs=segs):
                g_ = state["g"]
                j = g_ % 2
                src = AP(src_hbm.tensor, i * 128 * C, [[C, 128], [1, C]])
                dst = AP(d_rows.tensor, i * 128 * C, [[C, 128], [1, C]])

                @b.block.gpsimd
                def _(g, g_=g_, j=j):
                    if g_ >= 2:
                        sname, v = sel_ms[g_ - 2]
                        g.wait_ge(getattr(b, sname), v)
                    g.wait_ge(b.qs, b.cnt["qs"])
                    b.inc(g.dma_start(out=ia[j][:, 0:C], in_=src), "qg", 16)
                ld = ("qg", b.cnt["qg"])

                if mode == "select":
                    ve0 = b.cnt["ve"]
                    pe0 = b.cnt["pe"]
                    ac0 = b.cnt["ac"]

                    @b.block.vector
                    def _(v, g_=g_, j=j, ld=ld, ve0=ve0, pe0=pe0, segs=segs):
                        v.wait_ge(getattr(b, ld[0]), ld[1])
                        n = 0
                        for si, (c0, c1) in enumerate(segs):
                            for vv in range(16):
                                if n >= 8:
                                    v.wait_ge(b.pe, pe0 + n - 7)
                                b.inc(v.tensor_scalar(
                                    out=mbuf[n % 8][:, 0:c1 - c0],
                                    in0=ia[j][:, c0:c1], scalar1=float(vv),
                                    scalar2=lut128[:, vv:vv + 1],
                                    op0=ALU.is_equal, op1=ALU.mult), "ve")
                                n += 1

                    @b.block.tensor
                    def _(t, ve0=ve0, pe0=pe0, ac0=ac0, segs=segs):
                        n = 0
                        for si, (c0, c1) in enumerate(segs):
                            if si >= 1:
                                t.wait_ge(b.ac, ac0 + si)
                            w = c1 - c0
                            for vv in range(16):
                                t.wait_ge(b.ve, ve0 + n + 1)
                                last = None
                                for mc in range(0, w, 512):
                                    last = t.matmul(
                                        out=ps_sel[:, mc:min(mc + 512, w)],
                                        lhsT=ident[:],
                                        rhs=mbuf[n % 8][:, mc:min(mc + 512, w)],
                                        start=(vv == 0), stop=(vv == 15))
                                b.inc(last, "pe")
                                n += 1

                    @b.block.scalar
                    def _(sc, j=j, pe0=pe0, segs=segs):
                        for si, (c0, c1) in enumerate(segs):
                            sc.wait_ge(b.pe, pe0 + (si + 1) * 16)
                            b.inc(sc.activation(out=wa[j][:, c0:c1],
                                                in_=ps_sel[:, 0:c1 - c0],
                                                func=AF.Copy), "ac")
                    cm = ("ac", b.cnt["ac"])
                else:
                    @b.block.vector
                    def _(v, j=j, ld=ld, scol=scol, i=i, C=C):
                        v.wait_ge(getattr(b, ld[0]), ld[1])
                        sc = scol[0][:, i:i + 1]
                        sm = scol[1][:, i:i + 1]
                        b.inc(v.tensor_scalar(out=wa[j][:, 0:C], in0=ia[j][:, 0:C],
                                              scalar1=sc, scalar2=sm,
                                              op0=ALU.mult, op1=ALU.add), "ve")
                        v.wait_ge(b.ve, b.cnt["ve"])
                    cm = ("ve", b.cnt["ve"])

                @b.block.sync
                def _(q, j=j, cm=cm, C=C, dst=dst):
                    q.wait_ge(getattr(b, cm[0]), cm[1])
                    b.inc(q.dma_start(out=dst, in_=wa[j][:, 0:C]), "qs", 16)
                sel_ms.append(("qs", b.cnt["qs"]))
                b.pump_last = sel_ms[-1]
                state["g"] += 1
            b.pump_q.append(emit)
    return rows


def emit_tilings(b, tag, items, barrier=True):
    rmax = max(it[2] for it in items)
    if barrier:
        b.push()
    st = [b.sb(f"{tag}_s", [128, rmax], BF16) for _ in range(2)]
    til_ms = []
    out = {}
    for name, d_rows, R, C in items:
        nkt = C // 128
        nmb = R // 128
        d_til = b.dr(f"d_{name}T", [nmb, 128, nkt, 128], BF16)
        out[name] = d_til
        for kt in range(nkt):
            tj = len(til_ms) % 2

            @b.block.scalar
            def _(q, kt=kt, tj=tj, d_rows=d_rows, R=R, nt=len(til_ms)):
                if nt >= 2:
                    sname, v = til_ms[nt - 2]
                    q.wait_ge(getattr(b, sname), v)
                b.inc(q.dma_start_transpose(st[tj][:, 0:R],
                                            d_rows[:, kt * 128:(kt + 1) * 128]),
                      "qc", 16)
            tl = b.cnt["qc"]

            @b.block.sync
            def _(q, kt=kt, tj=tj, d_til=d_til, R=R, nkt=nkt, nmb=nmb, tl=tl):
                q.wait_ge(b.qc, tl)
                dst = AP(d_til.tensor, kt * 128,
                         [[nkt * 128, 128], [128 * nkt * 128, nmb], [1, 128]])
                b.inc(q.dma_start(
                    out=dst,
                    in_=st[tj].ap()[:, 0:R].rearrange("p (mb m) -> p mb m", m=128)),
                    "qs", 16)
            til_ms.append(("qs", b.cnt["qs"]))
    if barrier:
        b.pop()
    out["_ms"] = til_ms[-1] if til_ms else None
    return out


def prep_ffn_weights(b, tag, specs, lut128=None, scol_map=None):
    cmax = max(sp[3] for sp in specs)
    rmax = max(sp[2] for sp in specs)
    b.push()
    ia = [b.sb(f"{tag}_i", [128, cmax], BF16) for _ in range(2)]
    wa = [b.sb(f"{tag}_w", [128, cmax], BF16) for _ in range(2)]
    wb2 = [b.sb(f"{tag}_w2", [128, cmax], BF16) for _ in range(2)]
    ta = b.sb(f"{tag}_t", [128, cmax], BF16)
    tb2 = b.sb(f"{tag}_t2", [128, cmax], BF16)
    st = [b.sb(f"{tag}_s", [128, rmax], BF16) for _ in range(2)]
    sel_ms = []
    til_ms = []
    out = {}
    rows_of = {}
    gtile = 0

    for name, src_hbm, R, C, mode in specs:
        nr = R // 128
        d_rows = b.dr(f"d_{name}_rows", [R, C], BF16)
        rows_of[name] = d_rows
        src_t = src_hbm.rearrange("(n p) c -> n p c", p=128)
        dst_t = d_rows.rearrange("(n p) c -> n p c", p=128)
        scol = scol_map.get(name) if scol_map else None
        for i in range(nr):
            j = gtile % 2

            @b.block.gpsimd
            def _(g, i=i, j=j, gtile=gtile, src_t=src_t, C=C):
                if gtile >= 2:
                    sname, v = sel_ms[gtile - 2]
                    g.wait_ge(getattr(b, sname), v)
                b.inc(g.dma_start(out=ia[j][:, 0:C], in_=src_t[i]), "qg", 16)
            ld = ("qg", b.cnt["qg"])

            @b.block.vector
            def _(v, i=i, j=j, gtile=gtile, ld=ld, C=C, mode=mode, scol=scol):
                v.wait_ge(getattr(b, ld[0]), ld[1])
                if gtile >= 2:
                    sname, val = sel_ms[gtile - 2]
                    v.wait_ge(getattr(b, sname), val)
                sc = scol[0][:, i:i + 1] if scol is not None else None
                sm = scol[1][:, i:i + 1] if scol is not None else None
                _emit_select(b, v, "ve", ia[j][:, 0:C], wa[j][:, 0:C],
                             wb2[j][:, 0:C], ta[:, 0:C], tb2[:, 0:C],
                             lut128, mode, sc, sm)
            cm = ("ve", b.cnt["ve"])

            @b.block.sync
            def _(q, i=i, j=j, cm=cm, dst_t=dst_t, C=C):
                q.wait_ge(getattr(b, cm[0]), cm[1])
                b.inc(q.dma_start(out=dst_t[i], in_=wa[j][:, 0:C]), "qs", 16)
            sel_ms.append(("qs", b.cnt["qs"]))
            gtile += 1

    b.barrier()

    for name, src_hbm, R, C, mode in specs:
        d_rows = rows_of[name]
        nkt = C // 128
        nmb = R // 128
        d_til = b.dr(f"d_{name}T", [nmb, 128, nkt, 128], BF16)
        out[name] = d_til
        for kt in range(nkt):
            tj = len(til_ms) % 2

            @b.block.scalar
            def _(q, kt=kt, tj=tj, d_rows=d_rows, R=R, nt=len(til_ms)):
                if nt >= 2:
                    sname, v = til_ms[nt - 2]
                    q.wait_ge(getattr(b, sname), v)
                b.inc(q.dma_start_transpose(st[tj][:, 0:R],
                                            d_rows[:, kt * 128:(kt + 1) * 128]),
                      "qc", 16)
            tl = b.cnt["qc"]

            @b.block.sync
            def _(q, kt=kt, tj=tj, d_til=d_til, R=R, nkt=nkt, nmb=nmb, tl=tl):
                q.wait_ge(b.qc, tl)
                dst = AP(d_til.tensor, kt * 128,
                         [[nkt * 128, 128], [128 * nkt * 128, nmb], [1, 128]])
                b.inc(q.dma_start(
                    out=dst,
                    in_=st[tj].ap()[:, 0:R].rearrange("p (mb m) -> p mb m", m=128)),
                    "qs", 16)
            til_ms.append(("qs", b.cnt["qs"]))

    b.pop()
    return out


def prep_tiled(b, name, d_rows, R, C):
    nkt = C // 128
    nmb = R // 128
    d_til = b.dr(f"d_{name}T", [nmb, 128, nkt, 128], BF16)
    b.push()
    st = [b.sb(f"{name}_s", [128, R], BF16) for _ in range(2)]
    store_ms = []
    for kt in range(nkt):
        j = kt % 2

        @b.block.scalar
        def _(q, kt=kt, j=j):
            if kt >= 2:
                sname, v = store_ms[kt - 2]
                q.wait_ge(getattr(b, sname), v)
            b.inc(q.dma_start_transpose(st[j][:], d_rows[:, kt * 128:(kt + 1) * 128]),
                  "qc", 16)
            q.wait_ge(b.qc, b.cnt["qc"])
            dst = AP(d_til.tensor, kt * 128,
                     [[nkt * 128, 128], [128 * nkt * 128, nmb], [1, 128]])
            b.inc(q.dma_start(out=dst,
                              in_=st[j].ap().rearrange("p (mb m) -> p mb m", m=128)),
                  "qc", 16)
        store_ms.append(("qc", b.cnt["qc"]))
    b.pop()
    return d_til


def emit_fm_copy(b, name, d_rows, R, C, d_dst):
    b.push()
    st = [b.sb(f"{name}_s", [128, R], BF16) for _ in range(2)]
    dst_t = d_dst.rearrange("(n p) c -> n p c", p=128)
    store_ms = []
    fence = b.pump_fence()
    for kt in range(C // 128):
        j = kt % 2

        @b.block.scalar
        def _(q, kt=kt, j=j, fence=fence):
            if kt == 0 and fence is not None:
                q.wait_ge(getattr(b, fence[0]), fence[1])
            if kt >= 2:
                sname, v = store_ms[kt - 2]
                q.wait_ge(getattr(b, sname), v)
            b.inc(q.dma_start_transpose(st[j][:], d_rows[:, kt * 128:(kt + 1) * 128]),
                  "qc", 16)
            q.wait_ge(b.qc, b.cnt["qc"])
            b.inc(q.dma_start(out=dst_t[kt], in_=st[j][:]), "qc", 16)
        store_ms.append(("qc", b.cnt["qc"]))
    b.pop()



def emit_resid_ln(b, tag, x_src, u_src, x_dst, d_xn, pump=0):
    b.push()
    nt = T // 128
    xs = [b.sb(f"{tag}_x", [128, D], F32) for _ in range(2)]
    us = [b.sb(f"{tag}_u", [128, D], BF16) for _ in range(2)]
    xns = [b.sb(f"{tag}_xn", [128, D], BF16) for _ in range(2)]
    trash = b.sb(f"{tag}_tr", [128, D], F32)
    mcol = b.sb(f"{tag}_m", [128, 1], F32)
    vcol = b.sb(f"{tag}_v", [128, 1], F32)
    rcol = b.sb(f"{tag}_r", [128, 1], F32)
    scol = b.sb(f"{tag}_s", [128, 1], F32)
    x_t = x_src.rearrange("(n p) c -> n p c", p=128)
    xd_t = x_dst.rearrange("(n p) c -> n p c", p=128) if x_dst is not None else None
    xn_t = d_xn.rearrange("(n p) c -> n p c", p=128) if d_xn is not None else None
    store_ms = []

    for i in range(nt):
        j = i % 2

        @b.block.sync
        def _(q, i=i, j=j):
            if i >= 2:
                sname, v = store_ms[i - 2]
                q.wait_ge(getattr(b, sname), v)
            b.inc(q.dma_start(out=xs[j][:], in_=x_t[i]), "qs", 16)
            if u_src is not None:
                b.inc(q.dma_start_transpose(us[j][:],
                                            u_src[:, i * 128:(i + 1) * 128]), "qs", 16)
        ld = b.cnt["qs"]

        if u_src is not None:
            @b.block.vector
            def _(v, j=j, ld=ld):
                v.wait_ge(b.qs, ld)
                b.inc(v.tensor_add(out=xs[j][:], in0=xs[j][:], in1=us[j][:]), "ve")
                v.wait_ge(b.ve, b.cnt["ve"])
        xready = ("ve", b.cnt["ve"]) if u_src is not None else ("qs", ld)

        if d_xn is not None:
            @b.block.vector
            def _(v, j=j, xready=xready):
                v.wait_ge(getattr(b, xready[0]), xready[1])
                b.inc(v.tensor_reduce(out=mcol[:], in_=xs[j][:], axis=AXX, op=ALU.add),
                      "ve")
                v.wait_ge(b.ve, b.cnt["ve"])
                b.inc(v.tensor_scalar_mul(out=mcol[:], in0=mcol[:], scalar1=1.0 / D),
                      "ve")
                v.wait_ge(b.ve, b.cnt["ve"])
                b.inc(v.tensor_tensor(out=scol[:], in0=mcol[:], in1=mcol[:],
                                      op=ALU.mult), "ve")
                v.wait_ge(b.ve, b.cnt["ve"])

            @b.block.scalar
            def _(s, j=j, xready=xready):
                s.wait_ge(getattr(b, xready[0]), xready[1])
                b.inc(s.activation(out=trash[:], in_=xs[j][:], func=AF.Square,
                                   accum_out=vcol[:]), "ac")
            sq_ms = b.cnt["ac"]

            @b.block.vector
            def _(v, sq_ms=sq_ms):
                v.wait_ge(b.ac, sq_ms)
                b.inc(v.tensor_scalar(out=vcol[:], in0=vcol[:], scalar1=1.0 / D,
                                      scalar2=1e-5, op0=ALU.mult, op1=ALU.add), "ve")
                v.wait_ge(b.ve, b.cnt["ve"])
                b.inc(v.tensor_sub(out=vcol[:], in0=vcol[:], in1=scol[:]), "ve")
                v.wait_ge(b.ve, b.cnt["ve"])
            var_ms = b.cnt["ve"]

            @b.block.scalar
            def _(s, var_ms=var_ms):
                s.wait_ge(b.ve, var_ms)
                b.inc(s.activation(out=rcol[:], in_=vcol[:], func=AF.Ln), "ac")
                s.wait_ge(b.ac, b.cnt["ac"])
                b.inc(s.activation(out=rcol[:], in_=rcol[:], func=AF.Exp, scale=-0.5),
                      "ac")
            rstd_ms = b.cnt["ac"]

            @b.block.vector
            def _(v, j=j, rstd_ms=rstd_ms):
                v.wait_ge(b.ac, rstd_ms)
                b.inc(v.tensor_scalar(out=xns[j][:], in0=xs[j][:], scalar1=mcol[:],
                                      scalar2=rcol[:], op0=ALU.subtract, op1=ALU.mult),
                      "ve")

        done = ("ve", b.cnt["ve"]) if (d_xn is not None or u_src is not None) \
            else ("qs", ld)

        @b.block.gpsimd
        def _(g, i=i, j=j, done=done):
            g.wait_ge(getattr(b, done[0]), done[1])
            if x_dst is not None:
                b.inc(g.dma_start(out=xd_t[i], in_=xs[j][:]), "qg", 16)
            if d_xn is not None:
                b.inc(g.dma_start(out=xn_t[i], in_=xns[j][:]), "qg", 16)
        store_ms.append(("qg", b.cnt["qg"]))
        b.pump(pump)

    b.pop()



def emit_proj_in(b, d_w_rows, d_xn, bias_cols, d_out):
    b.push()
    wstr = [b.sb("pi_w", [128, D], BF16) for _ in range(DT)]
    xstr = [b.sb("pi_x", [128, T], BF16) for _ in range(DT)]
    os_ = [b.sb("pi_o", [128, 512], BF16) for _ in range(2)]
    o_t = d_out.rearrange("(n p) c -> n p c", p=128)
    stream_stripes_T(b, wstr, d_w_rows)
    load_stripes(b, xstr, d_xn)
    lw = ("qs", b.cnt["qs"])

    def evac(mb, tc, psum):
        oj = (mb * 4 + tc) % 2

        @b.block.vector
        def _(v, mb=mb, tc=tc, oj=oj, psum=psum, pe=b.cnt["pe"]):
            v.wait_ge(b.pe, pe)
            if mb * 4 + tc >= 2:
                v.wait_ge(b.qg, b.cnt["qg"] - 16)
            b.inc(v.tensor_scalar_add(out=os_[oj][:], in0=psum[:],
                                      scalar1=bias_cols[:, mb:mb + 1]), "ve")

        @b.block.gpsimd
        def _(g, mb=mb, tc=tc, oj=oj, ve=b.cnt["ve"]):
            g.wait_ge(b.ve, ve)
            b.inc(g.dma_start(out=o_t[mb][:, tc * 512:(tc + 1) * 512], in_=os_[oj][:]),
                  "qg", 16)
        return ("ve", b.cnt["ve"])

    mm_grid(b, DT, 4, DT,
            lambda kt, mb: wstr[kt][:, mb * 128:(mb + 1) * 128],
            lambda kt, tc: xstr[kt][:, tc * 512:(tc + 1) * 512],
            evac, pre_wait=[lw], pump_per=1)
    b.pop()


def emit_proj_out(b, d_w_rows, d_y_rows, bias_cols, d_xn, d_out):
    b.push()
    wstr = [b.sb("po_w", [128, D], BF16) for _ in range(DT)]
    ystr = [b.sb("po_y", [128, T], BF16) for _ in range(DT)]
    xn_t = [b.sb("po_xn", [128, T], BF16) for _ in range(2)]
    os_ = [b.sb("po_o", [128, 512], BF16) for _ in range(2)]
    o_t = d_out.rearrange("(n p) c -> n p c", p=128)
    stream_stripes_T(b, wstr, d_w_rows)
    load_stripes(b, ystr, d_y_rows)
    lw = ("qs", b.cnt["qs"])
    xn_ready = [None, None]
    xn_free = {}

    def evac(mb, tc, psum):
        n = mb * 4 + tc
        oj = n % 2
        if tc == 0:
            @b.block.scalar
            def _(q, mb=mb):
                if mb >= 2:
                    sname, v = xn_free[mb - 2]
                    q.wait_ge(getattr(b, sname), v)
                b.inc(q.dma_start(
                    out=xn_t[mb % 2][:],
                    in_=d_xn.rearrange("(a p) c -> a p c", p=128)[mb]), "qc", 16)
            ld = b.cnt["qc"]

            @b.block.vector
            def _(v, mb=mb, ld=ld):
                v.wait_ge(b.qc, ld)
                b.inc(v.tensor_scalar_add(out=xn_t[mb % 2][:], in0=xn_t[mb % 2][:],
                                          scalar1=bias_cols[:, mb:mb + 1]), "ve")
                v.wait_ge(b.ve, b.cnt["ve"])

        @b.block.vector
        def _(v, mb=mb, tc=tc, oj=oj, psum=psum, pe=b.cnt["pe"], n=n):
            v.wait_ge(b.pe, pe)
            if n >= 2:
                v.wait_ge(b.qg, b.cnt["qg"] - 16)
            b.inc(v.tensor_add(out=os_[oj][:],
                               in0=xn_t[mb % 2][:, tc * 512:(tc + 1) * 512],
                               in1=psum[:]), "ve")
        ms = ("ve", b.cnt["ve"])
        if tc == 3:
            xn_free[mb] = ms

        @b.block.gpsimd
        def _(g, mb=mb, tc=tc, oj=oj, ve=b.cnt["ve"]):
            g.wait_ge(b.ve, ve)
            b.inc(g.dma_start(out=o_t[mb][:, tc * 512:(tc + 1) * 512], in_=os_[oj][:]),
                  "qg", 16)
        return ms

    mm_grid(b, DT, 4, DT,
            lambda kt, mb: wstr[kt][:, mb * 128:(mb + 1) * 128],
            lambda kt, tc: ystr[kt][:, tc * 512:(tc + 1) * 512],
            evac, pre_wait=[lw], pump_per=1)
    b.pop()



def emit_scan(b, bw_in, a_in, cw_in, dp_cols, onesr, d_bw_rows, d_xp, d_zpad, d_y0):
    b.push()
    nc = b.nc
    xstr = [b.sb("sc_x", [128, T], BF16) for _ in range(DT)]
    bwT = [b.sb("sc_bwT", [128, S], BF16) for _ in range(DT)]
    ident = b.sb("sc_id", [128, 128], BF16)
    identf = b.sb("sc_idf", [128, 128], F32)
    pcolf = b.sb("sc_pcol", [128, 1], F32)
    i16a = b.sb("sc_i16a", [16, 128], F32)
    delta_bf = b.sb("sc_deltab", [16, T], BF16)
    dl1 = b.sb("sc_dl1", [16, 512], F32)
    dl2 = b.sb("sc_dl2", [16, 512], F32)
    dl3 = b.sb("sc_dl3", [16, 512], F32)
    ones16 = b.sb("sc_ones16", [16, 1], F32)
    r16 = b.sb("sc_r16", [16, 128], BF16)
    r16f = b.sb("sc_r16f", [16, 128], F32)
    drep = b.sb("sc_drep", [128, T], F32)
    acol = b.sb("sc_acol", [128, DT * S], F32)
    lna = b.sb("sc_lna", [128, DT * S], F32)
    eiota = b.sb("sc_ei", [128, 512], F32)
    eio2 = b.sb("sc_ei2", [128, 512], F32)
    ptmp = [b.sb("sc_ptmp", [128, 512], F32) for _ in range(2)]
    pstr = b.sb("sc_pstr", [128, DT * 512], BF16)
    zt = [b.sb("sc_zt", [128, 512], BF16) for _ in range(2)]
    zhead = b.sb("sc_zh", [128, 31], BF16)
    zsh = [b.sb("sc_zsh", [128, T], BF16) for _ in range(4)]
    cw_fm = b.sb("sc_cwfm", [128, S], BF16)
    cwT = b.sb("sc_cwT", [16, T], BF16)
    cwrep = b.sb("sc_cwrep", [128, T], BF16)
    ys = [b.sb("sc_y", [128, 512], BF16) for _ in range(2)]
    ps_tb = b.ps("sc_pstb", [16, 128], BF16)
    ps_r = b.ps("sc_psr", [128, 512])
    ps_d = ps_r[0:16, :]
    y_t = d_y0.rearrange("(n p) c -> n p c", p=128)
    xp_t = d_xp.rearrange("(n p) c -> n p c", p=128)

    load_stripes(b, xstr, d_xp)
    stream_stripes_T(b, bwT, d_bw_rows)
    xp_ld = b.cnt["qs"]

    @b.block.gpsimd
    def _(g):
        g.iota(i16a[:], pattern=[[0, 8], [1, 16]], base=0, channel_multiplier=0,
               allow_small_or_imprecise_dtypes=True)
        g.iota(eiota[:], pattern=[[8, 4], [0, 8], [0, 16]], base=7,
               channel_multiplier=0, allow_small_or_imprecise_dtypes=True)
        g.iota(eio2[:], pattern=[[0, 4], [1, 8], [0, 16]], base=0,
               channel_multiplier=0, allow_small_or_imprecise_dtypes=True)
        i = g.iota(pcolf[:], pattern=[[0, 1]], base=0, channel_multiplier=1,
                   allow_small_or_imprecise_dtypes=True)
        b.inc(i, "po")

    @b.block.vector
    def _(v, po=b.cnt["po"]):
        v.wait_ge(b.po, po)
        b.inc(v.memset(ones16[:], 1.0), "ve")
        b.inc(v.memset(zhead[:], 0.0), "ve")
        b.inc(v.tensor_sub(out=eiota[:], in0=eiota[:], in1=eio2[:]), "ve")

    idcol = b.sb("sc_idcol", [128, 128], F32)

    @b.block.gpsimd
    def _(g):
        i = g.iota(idcol[:], pattern=[[1, 128]], base=0, channel_multiplier=0,
                   allow_small_or_imprecise_dtypes=True)
        b.inc(i, "po")

    @b.block.vector
    def _(v, po=b.cnt["po"]):
        v.wait_ge(b.po, po)
        b.inc(v.tensor_scalar(out=identf[:], in0=idcol[:], scalar1=pcolf[:],
                              scalar2=None, op0=ALU.is_equal), "ve")
        v.wait_ge(b.ve, b.cnt["ve"])
        b.inc(v.tensor_copy(out=ident[:], in_=identf[:]), "ve")
        b.inc(v.tensor_scalar(out=r16f[:], in0=i16a[:], scalar1=pcolf[0:16, 0:1],
                              scalar2=None, op0=ALU.is_equal), "ve")
        v.wait_ge(b.ve, b.cnt["ve"])
        b.inc(v.tensor_copy(out=r16[:], in_=r16f[:]), "ve")
    consts_ms = ("ve", b.cnt["ve"])

    for tc in range(4):
        @b.block.tensor
        def _(t, tc=tc, ve=b.cnt["ve"]):
            t.wait_ge(b.qs, xp_ld)
            if tc >= 1:
                t.wait_ge(b.ve, ve)
            last = None
            for kt in range(DT):
                last = t.matmul(out=ps_d, lhsT=bwT[kt][:],
                                rhs=xstr[kt][:, tc * 512:(tc + 1) * 512],
                                start=(kt == 0), stop=(kt == DT - 1))
            b.inc(last, "pe")

        @b.block.vector
        def _(v, tc=tc, pe=b.cnt["pe"], cms=consts_ms):
            v.wait_ge(b.pe, pe)
            v.wait_ge(getattr(b, cms[0]), cms[1])
            b.inc(v.tensor_relu(out=dl2[:], in_=ps_d), "ve")
            v.wait_ge(b.ve, b.cnt["ve"])
            b.inc(v.scalar_tensor_tensor(out=dl1[:], in0=dl2[:], scalar=2.0,
                                         in1=ps_d, op0=ALU.mult,
                                         op1=ALU.subtract), "ve")
            v.wait_ge(b.ve, b.cnt["ve"])

        @b.block.scalar
        def _(s, tc=tc, ve=b.cnt["ve"]):
            s.wait_ge(b.ve, ve)
            b.inc(s.activation(out=dl3[:], in_=dl1[:], func=AF.Exp, scale=-1.0), "ac")
            s.wait_ge(b.ac, b.cnt["ac"])
            b.inc(s.activation(out=dl3[:], in_=dl3[:], func=AF.Ln, bias=ones16[:]),
                  "ac")

        @b.block.vector
        def _(v, tc=tc, ac=b.cnt["ac"]):
            v.wait_ge(b.ac, ac)
            b.inc(v.tensor_add(out=dl2[:], in0=dl2[:], in1=dl3[:]), "ve")
            v.wait_ge(b.ve, b.cnt["ve"])
            b.inc(v.tensor_copy(out=delta_bf[:, tc * 512:(tc + 1) * 512], in_=dl2[:]),
                  "ve")
            v.wait_ge(b.ve, b.cnt["ve"])
        b.pump(1)

    for tc in range(4):
        @b.block.tensor
        def _(t, tc=tc, ve=b.cnt["ve"]):
            t.wait_ge(b.ve, ve)
            b.inc(t.matmul(out=ps_r[:], lhsT=r16[:],
                           rhs=delta_bf[:, tc * 512:(tc + 1) * 512],
                           start=True, stop=True), "pe")

        @b.block.vector
        def _(v, tc=tc, pe=b.cnt["pe"]):
            v.wait_ge(b.pe, pe)
            b.inc(v.tensor_copy(out=drep[:, tc * 512:(tc + 1) * 512], in_=ps_r[:]),
                  "ve")

    @b.block.gpsimd
    def _(g):
        for k in range(DT):
            src = AP(a_in.tensor, k * 128 * S, [[S, 128], [1, S]])
            b.inc(g.dma_start(out=acol[:, k * S:(k + 1) * S], in_=src,
                              allow_slow_non_contiguous=True), "qg", 16)

    @b.block.scalar
    def _(sc, qg=b.cnt["qg"]):
        sc.wait_ge(b.qg, qg)
        b.inc(sc.activation(out=acol[:], in_=acol[:], func=AF.Sigmoid), "ac")
        sc.wait_ge(b.ac, b.cnt["ac"])
        b.inc(sc.activation(out=lna[:], in_=acol[:], func=AF.Ln), "ac")
    lna_ms = b.cnt["ac"]

    for k in range(DT):
        pj = k % 2

        @b.block.vector
        def _(v, k=k, pj=pj, ac=b.cnt["ac"], lna_ms=lna_ms):
            v.wait_ge(b.ac, max(ac, lna_ms))
            lnab = lna.ap()[:, k * S:(k + 1) * S].unsqueeze(1).broadcast_to((128, 32, S))
            b.inc(v.tensor_tensor(out=ptmp[pj].ap().rearrange("p (l s) -> p l s", s=S),
                                  in0=eiota.ap().rearrange("p (l s) -> p l s", s=S),
                                  in1=lnab, op=ALU.mult), "ve")

        @b.block.scalar
        def _(sc, k=k, pj=pj, ve=b.cnt["ve"]):
            sc.wait_ge(b.ve, ve)
            b.inc(sc.activation(out=pstr[:, k * 512:(k + 1) * 512], in_=ptmp[pj][:],
                                func=AF.Exp), "ac")
        b.pump(1)
    pows_ms = ("ac", b.cnt["ac"])

    @b.block.gpsimd
    def _(g, cms=consts_ms):
        g.wait_ge(getattr(b, cms[0]), cms[1])
        for gi in range(4):
            b.inc(g.dma_start(out=d_zpad[gi, :, 0:31], in_=zhead[:]), "qg", 16)

    ps_y = [b.ps("sc_psy", [128, 512]) for _ in range(2)]
    z_ms = []
    n = 0
    for gi in range(4):
        for tc in range(4):
            pj = n % 2

            @b.block.tensor
            def _(t, gi=gi, tc=tc, pj=pj, n=n, pm=pows_ms):
                if n == 0:
                    t.wait_ge(getattr(b, pm[0]), pm[1])
                if n >= 2:
                    sname, v = z_ms[n - 2]
                    t.wait_ge(getattr(b, sname), v)
                last = None
                for kt in range(DT):
                    last = t.matmul(
                        out=ps_y[pj][:],
                        lhsT=pstr[:, kt * 512 + gi * 128: kt * 512 + (gi + 1) * 128],
                        rhs=xstr[kt][:, tc * 512:(tc + 1) * 512],
                        start=(kt == 0), stop=(kt == DT - 1))
                b.inc(last, "pe")

            @b.block.vector
            def _(v, gi=gi, tc=tc, pj=pj, n=n, pe=b.cnt["pe"]):
                v.wait_ge(b.pe, pe)
                if n >= 2:
                    v.wait_ge(b.qg, b.cnt["qg"] - 16)
                b.inc(v.tensor_mul(out=zt[n % 2][:], in0=ps_y[pj][:],
                                   in1=drep[:, tc * 512:(tc + 1) * 512]), "ve")
            zv = b.cnt["ve"]

            @b.block.gpsimd
            def _(g, gi=gi, tc=tc, n=n, zv=zv):
                g.wait_ge(b.ve, zv)
                b.inc(g.dma_start(out=d_zpad[gi, :, 31 + tc * 512: 31 + (tc + 1) * 512],
                                  in_=zt[n % 2][:]), "qg", 16)
            z_ms.append(("ve", zv))
            b.pump(1)
            n += 1
    zstores = ("qg", b.cnt["qg"])

    for k in range(DT):
        @b.block.gpsimd
        def _(g, k=k, pe=b.cnt["pe"]):
            if k >= 1:
                g.wait_ge(b.pe, pe)
            src = AP(cw_in.tensor, k * 128 * S, [[S, 128], [1, S]])
            b.inc(g.dma_start(out=cw_fm[:], in_=src,
                              allow_slow_non_contiguous=True), "qg", 16)

        @b.block.tensor
        def _(t, k=k, qg=b.cnt["qg"], ve=b.cnt["ve"]):
            t.wait_ge(b.qg, qg)
            if k >= 1:
                t.wait_ge(b.ve, ve)
            b.inc(t.transpose(ps_tb[:], cw_fm[:], ident[:]), "pe")

        @b.block.vector
        def _(v, k=k, pe=b.cnt["pe"]):
            v.wait_ge(b.pe, pe)
            b.inc(v.tensor_copy(out=cwT[:, k * 128:(k + 1) * 128], in_=ps_tb[:]), "ve")

    for tc in range(4):
        @b.block.tensor
        def _(t, tc=tc, ve=b.cnt["ve"]):
            t.wait_ge(b.ve, ve)
            b.inc(t.matmul(out=ps_r[:], lhsT=r16[:],
                           rhs=cwT[:, tc * 512:(tc + 1) * 512],
                           start=True, stop=True), "pe")

        @b.block.vector
        def _(v, tc=tc, pe=b.cnt["pe"]):
            v.wait_ge(b.pe, pe)
            b.inc(v.tensor_copy(out=cwrep[:, tc * 512:(tc + 1) * 512], in_=ps_r[:]),
                  "ve")

    @b.block.sync
    def _(q, zs=zstores):
        q.wait_ge(getattr(b, zs[0]), zs[1])
        for gi in range(4):
            src = AP(d_zpad.tensor, gi * 128 * ZROW + (24 - 8 * gi),
                     [[16 * ZROW + 1, 8], [ZROW, 16], [1, T]])
            b.inc(q.dma_start(out=zsh[gi][:], in_=src), "qs", 16)
    zsh_ld = ("qs", b.cnt["qs"])

    def evac(mb, tc, psum):
        n2 = mb * 4 + tc
        oj = n2 % 2

        @b.block.vector
        def _(v, mb=mb, tc=tc, oj=oj, psum=psum, pe=b.cnt["pe"], n2=n2):
            v.wait_ge(b.pe, pe)
            if n2 >= 2:
                v.wait_ge(b.qg, b.cnt["qg"] - 16)
            b.inc(v.scalar_tensor_tensor(out=ys[oj][:],
                                         in0=xstr[mb][:, tc * 512:(tc + 1) * 512],
                                         scalar=dp_cols[:, mb:mb + 1], in1=psum[:],
                                         op0=ALU.mult, op1=ALU.add), "ve")
        ms = ("ve", b.cnt["ve"])

        @b.block.gpsimd
        def _(g, mb=mb, tc=tc, oj=oj, ve=b.cnt["ve"]):
            g.wait_ge(b.ve, ve)
            b.inc(g.dma_start(out=y_t[mb][:, tc * 512:(tc + 1) * 512], in_=ys[oj][:]),
                  "qg", 16)
        return ms

    mm_grid(b, DT, 4, 4,
            lambda gi, mb: cwrep[:, mb * 128:(mb + 1) * 128],
            lambda gi, tc: zsh[gi][:, tc * 512:(tc + 1) * 512],
            evac, pre_wait=[zsh_ld], pump_per=1)
    b.pop()



def emit_ffn(b, tag, d_wgT, d_wuT, d_wdT, nb, d_xn, d_s, d_u, down_wait=None):
    for tp in range(2):
        b.push()
        wg = [b.sb(f"{tag}_wg", [128, D], BF16) for _ in range(2)]
        wu = [b.sb(f"{tag}_wu", [128, D], BF16) for _ in range(2)]
        xnf = [b.sb(f"{tag}_xn", [128, 1024], BF16) for _ in range(DT)]
        hsb = b.sb(f"{tag}_h", [128, nb * 1024], BF16)
        sg_t = b.sb(f"{tag}_sg", [128, 1024], BF16)
        ps_g = [b.ps(f"{tag}_psg", [128, 512]) for _ in range(2)]
        ps_u = [b.ps(f"{tag}_psu", [128, 512]) for _ in range(2)]
        s_rows = d_s.rearrange("(n p) c -> n p c", p=128)
        u_rows = d_u.rearrange("(n p) c -> n p c", p=128)

        @b.block.sync
        def _(q):
            xt = d_xn.rearrange("(a p) c -> a p c", p=128)
            for k in range(DT):
                b.inc(q.dma_start(out=xnf[k][:],
                                  in_=xt[k][:, tp * 1024:(tp + 1) * 1024]), "qs", 16)

        qs0 = b.cnt["qs"]
        pe0 = b.cnt["pe"]
        ac0 = b.cnt["ac"]
        ve0 = b.cnt["ve"]

        @b.block.sync
        def _(q):
            for ib in range(nb):
                if ib >= 2:
                    q.wait_ge(b.ve, ve0 + (ib - 1) * 2)
                b.inc(q.dma_start(out=wg[ib % 2][:],
                                  in_=d_wgT[ib].rearrange("p k m -> p (k m)")), "qs", 16)
                b.inc(q.dma_start(out=wu[ib % 2][:],
                                  in_=d_wuT[ib].rearrange("p k m -> p (k m)")), "qs", 16)

        @b.block.tensor
        def _(t):
            for ib in range(nb):
                t.wait_ge(b.qs, qs0 + (ib + 1) * 32)
                if ib >= 1:
                    t.wait_ge(b.ve, ve0 + ib * 2)
                for w, psl in ((wg[ib % 2], ps_g), (wu[ib % 2], ps_u)):
                    for nck in range(2):
                        last = None
                        for kt in range(DT):
                            last = t.matmul(
                                out=psl[nck][:], lhsT=w[:, kt * 128:(kt + 1) * 128],
                                rhs=xnf[kt][:, nck * 512:(nck + 1) * 512],
                                start=(kt == 0), stop=(kt == DT - 1))
                    b.inc(last, "pe")

        @b.block.scalar
        def _(sc):
            for ib in range(nb):
                sc.wait_ge(b.pe, pe0 + 2 * ib + 1)
                b.inc(sc.activation(out=sg_t[:, 0:512], in_=ps_g[0][:], func=AF.Silu),
                      "ac")
                b.inc(sc.activation(out=sg_t[:, 512:1024], in_=ps_g[1][:],
                                    func=AF.Silu), "ac")

        @b.block.vector
        def _(v):
            for ib in range(nb):
                v.wait_ge(b.ac, ac0 + 2 * ib + 2)
                v.wait_ge(b.pe, pe0 + 2 * ib + 2)
                b.inc(v.tensor_mul(out=hsb[:, ib * 1024: ib * 1024 + 512],
                                   in0=sg_t[:, 0:512], in1=ps_u[0][:]), "ve")
                b.inc(v.tensor_mul(out=hsb[:, ib * 1024 + 512: (ib + 1) * 1024],
                                   in0=sg_t[:, 512:1024], in1=ps_u[1][:]), "ve")
        assert b.cnt["pe"] == pe0 + 2 * nb and b.cnt["ve"] == ve0 + 2 * nb

        wd = [b.sb(f"{tag}_wd", [128, nb * 128], BF16) for _ in range(2)]
        sst = [b.sb(f"{tag}_ss", [128, 1024], BF16) for _ in range(2)]
        g_t = b.sb(f"{tag}_g", [128, 1024], BF16)
        u_t = [b.sb(f"{tag}_u", [128, 1024], BF16) for _ in range(2)]
        ps_dn = [b.ps(f"{tag}_psd", [128, 512]) for _ in range(2)]
        qsD = b.cnt["qs"]
        peD = b.cnt["pe"]
        acD = b.cnt["ac"]
        veD = b.cnt["ve"]
        qgD = b.cnt["qg"]

        @b.block.sync
        def _(q):
            for eb in range(DT):
                if eb == 0 and down_wait is not None:
                    q.wait_ge(getattr(b, down_wait[0]), down_wait[1])
                if eb >= 2:
                    q.wait_ge(b.ve, veD + eb - 1)
                b.inc(q.dma_start(out=wd[eb % 2][:],
                                  in_=d_wdT[eb].rearrange("p k m -> p (k m)")), "qs", 16)
                b.inc(q.dma_start(out=sst[eb % 2][:],
                                  in_=s_rows[eb][:, tp * 1024:(tp + 1) * 1024]), "qs", 16)

        @b.block.tensor
        def _(t):
            for eb in range(DT):
                if eb == 0 and down_wait is not None:
                    t.wait_ge(getattr(b, down_wait[0]), down_wait[1])
                t.wait_ge(b.qs, qsD + (eb + 1) * 32)
                if eb >= 1:
                    t.wait_ge(b.ve, veD + eb)
                for nck in range(2):
                    last = None
                    for kb in range(nb):
                        last = t.matmul(
                            out=ps_dn[nck][:], lhsT=wd[eb % 2][:, kb * 128:(kb + 1) * 128],
                            rhs=hsb[:, kb * 1024 + nck * 512: kb * 1024 + (nck + 1) * 512],
                            start=(kb == 0), stop=(kb == nb - 1))
                    b.inc(last, "pe")

        @b.block.scalar
        def _(sc):
            for eb in range(DT):
                sc.wait_ge(b.pe, peD + 2 * eb + 2)
                b.inc(sc.activation(out=g_t[:, 0:512], in_=ps_dn[0][:], func=AF.Silu),
                      "ac")
                b.inc(sc.activation(out=g_t[:, 512:1024], in_=ps_dn[1][:],
                                    func=AF.Silu), "ac")

        @b.block.vector
        def _(v):
            for eb in range(DT):
                v.wait_ge(b.ac, acD + 2 * eb + 2)
                v.wait_ge(b.qs, qsD + (eb + 1) * 32)
                if eb >= 2:
                    v.wait_ge(b.qg, qgD + (eb - 1) * 16)
                b.inc(v.tensor_mul(out=u_t[eb % 2][:], in0=g_t[:], in1=sst[eb % 2][:]),
                      "ve")

        @b.block.gpsimd
        def _(g):
            for eb in range(DT):
                g.wait_ge(b.ve, veD + eb + 1)
                b.inc(g.dma_start(out=u_rows[eb][:, tp * 1024:(tp + 1) * 1024],
                                  in_=u_t[eb % 2][:]), "qg", 16)
        assert b.cnt["pe"] == peD + 2 * DT and b.cnt["ve"] == veD + DT

        b.pop()



def emit_cos_table(b, cos_t, tag):
    b.push()
    fcol = b.sb(f"{tag}_f", [128, 1], F32)
    iot = b.sb(f"{tag}_io", [128, T], F32)
    uu = b.sb(f"{tag}_uu", [128, T], F32)
    ki = b.sb(f"{tag}_ki", [128, T], I32)
    kf = b.sb(f"{tag}_kf", [128, T], F32)

    @b.block.gpsimd
    def _(g):
        g.iota(iot[:], pattern=[[1, T]], base=0, channel_multiplier=0,
               allow_small_or_imprecise_dtypes=True)
        i = g.iota(fcol[:], pattern=[[0, 1]], base=0, channel_multiplier=1,
                   allow_small_or_imprecise_dtypes=True)
        b.inc(i, "po")

    hicol = b.sb(f"{tag}_hi", [128, 1], F32)

    @b.block.vector
    def _(v, po=b.cnt["po"]):
        v.wait_ge(b.po, po)
        b.inc(v.tensor_scalar(out=hicol[:], in0=fcol[:], scalar1=64.0,
                              scalar2=None, op0=ALU.is_ge), "ve")
        v.wait_ge(b.ve, b.cnt["ve"])
        b.inc(v.scalar_tensor_tensor(out=fcol[:], in0=hicol[:], scalar=-64.0,
                                     in1=fcol[:], op0=ALU.mult, op1=ALU.add), "ve")
        v.wait_ge(b.ve, b.cnt["ve"])
        b.inc(v.tensor_scalar_mul(out=fcol[:], in0=fcol[:],
                                  scalar1=float(-2.0 * LN1E4 / HD)), "ve")
        v.wait_ge(b.ve, b.cnt["ve"])

    @b.block.scalar
    def _(s, ve=b.cnt["ve"]):
        s.wait_ge(b.ve, ve)
        b.inc(s.activation(out=fcol[:], in_=fcol[:], func=AF.Exp), "ac")

    @b.block.vector
    def _(v, ac=b.cnt["ac"]):
        v.wait_ge(b.ac, ac)
        b.inc(v.tensor_scalar(out=uu[:], in0=iot[:], scalar1=fcol[:],
                              scalar2=float(1.0 / (2 * PI)),
                              op0=ALU.mult, op1=ALU.mult), "ve")
        v.wait_ge(b.ve, b.cnt["ve"])
        b.inc(v.tensor_copy(out=ki[:], in_=uu[:]), "ve")
        v.wait_ge(b.ve, b.cnt["ve"])
        b.inc(v.tensor_copy(out=kf[:], in_=ki[:]), "ve")
        v.wait_ge(b.ve, b.cnt["ve"])
        b.inc(v.tensor_sub(out=uu[:], in0=uu[:], in1=kf[:]), "ve")
        v.wait_ge(b.ve, b.cnt["ve"])

    @b.block.scalar
    def _(s, ve=b.cnt["ve"]):
        s.wait_ge(b.ve, ve)
        b.inc(s.activation(out=kf[:], in_=uu[:], func=AF.Sin, scale=float(PI)), "ac")
        s.wait_ge(b.ac, b.cnt["ac"])
        b.inc(s.activation(out=kf[:], in_=kf[:], func=AF.Square), "ac")

    @b.block.vector
    def _(v, ac=b.cnt["ac"]):
        v.wait_ge(b.ac, ac)
        b.inc(v.tensor_scalar(out=cos_t[:], in0=kf[:], scalar1=-2.0, scalar2=1.0,
                              op0=ALU.mult, op1=ALU.add), "ve")
    b.pop()


def emit_rope(b, dst128, cos_t, rp1, sgncol, wait):
    @b.block.vector
    def _(v, wait=wait):
        for sname, val in wait:
            v.wait_ge(getattr(b, sname), val)
        for c0 in (0, 1024):
            b.inc(v.tensor_copy(out=rp1[0:64, :], in_=dst128[64:128, c0:c0 + 1024]),
                  "ve")
            b.inc(v.tensor_copy(out=rp1[64:128, :], in_=dst128[0:64, c0:c0 + 1024]),
                  "ve")
            v.wait_ge(b.ve, b.cnt["ve"])
            b.inc(v.scalar_tensor_tensor(out=rp1[:], in0=rp1[:], scalar=sgncol[:],
                                         in1=dst128[:, c0:c0 + 1024],
                                         op0=ALU.mult, op1=ALU.add), "ve")
            v.wait_ge(b.ve, b.cnt["ve"])
            b.inc(v.tensor_mul(out=dst128[:, c0:c0 + 1024], in0=rp1[:],
                               in1=cos_t[:, c0:c0 + 1024]), "ve")
            v.wait_ge(b.ve, b.cnt["ve"])


def emit_l1_attn(b, d_xn1, wq_in, wk_in, wv_in, wo_in, onesb, onesr, d_q, d_s1,
                 l1sel=None):
    b.push()
    k_fm = b.sb("at_kfm", [128, NKV * T], BF16)
    v_tm = b.sb("at_vtm", [128, 16 * 512], BF16)
    qt = [b.sb("at_q", [128, T], BF16) for _ in range(2)]
    q_rows = d_q.rearrange("(n p) c -> n p c", p=128)
    s_rows = d_s1.rearrange("(n p) c -> n p c", p=128)

    b.push()
    xstr = [b.sb("at_x", [128, T], BF16) for _ in range(DT)]
    cos_t = b.sb("at_cos", [128, T], F32)
    cos_k = b.sb("at_cosk", [128, T], F32)
    rp1 = b.sb("at_rp1", [128, 1024], F32)
    sgncol = b.sb("at_sgn", [128, 1], F32)

    @b.block.vector
    def _(v):
        b.inc(v.memset(sgncol[0:64, :], -1.0), "ve")
        b.inc(v.memset(sgncol[64:128, :], 1.0), "ve")
    emit_cos_table(b, cos_t, "ct")

    @b.block.vector
    def _(v, ve=b.cnt["ve"]):
        v.wait_ge(b.ve, ve)
        b.inc(v.tensor_scalar_mul(out=cos_k[:], in0=cos_t[:],
                                  scalar1=float(HD ** -0.5)), "ve")
        v.wait_ge(b.ve, b.cnt["ve"])
    load_stripes(b, xstr, d_xn1)
    xn_ld = ("qs", b.cnt["qs"])

    b.push()
    wstr = [b.sb("at_w", [128, D], BF16) for _ in range(DT)]
    cast_load_stripes(b, wstr, wq_in)
    w_ld = ("qg", b.cnt["qg"])
    q_free = {}

    def evac_q(mb, tc, psum):
        j = mb % 2

        @b.block.vector
        def _(v_, mb=mb, tc=tc, j=j, psum=psum, pe=b.cnt["pe"]):
            v_.wait_ge(b.pe, pe)
            if tc == 0 and mb >= 2:
                sname, v = q_free[mb - 2]
                v_.wait_ge(getattr(b, sname), v)
            b.inc(v_.tensor_copy(out=qt[j][:, tc * 512:(tc + 1) * 512], in_=psum[:]),
                  "ve")
        ms = ("ve", b.cnt["ve"])
        if tc == 3:
            emit_rope(b, qt[j].ap(), cos_t.ap(), rp1, sgncol, [ms])

            @b.block.gpsimd
            def _(g, mb=mb, j=j, ve=b.cnt["ve"]):
                g.wait_ge(b.ve, ve)
                b.inc(g.dma_start(out=q_rows[mb], in_=qt[j][:]), "qg", 16)
            q_free[mb] = ("qg", b.cnt["qg"])
            return ("ve", b.cnt["ve"])
        return ms

    mm_grid(b, NH, 4, DT,
            lambda kt, mb: wstr[kt][:, mb * 128:(mb + 1) * 128],
            lambda kt, tc: xstr[kt][:, tc * 512:(tc + 1) * 512],
            evac_q, pre_wait=[xn_ld, w_ld])
    b.pop()

    b.push()
    kv_w = [b.sb("at_kw", [128, 512], BF16) for _ in range(DT)]
    cast_load_stripes(b, kv_w, wk_in)
    kw_ld = ("qg", b.cnt["qg"])

    def evac_k(mb, tc, psum):
        @b.block.vector
        def _(v_, mb=mb, tc=tc, psum=psum, pe=b.cnt["pe"]):
            v_.wait_ge(b.pe, pe)
            b.inc(v_.tensor_copy(out=k_fm[:, mb * T + tc * 512: mb * T + (tc + 1) * 512],
                                 in_=psum[:]), "ve")
        ms = ("ve", b.cnt["ve"])
        if tc == 3:
            emit_rope(b, k_fm.ap()[:, mb * T:(mb + 1) * T], cos_k.ap(), rp1, sgncol, [ms])
            return ("ve", b.cnt["ve"])
        return ms

    mm_grid(b, NKV, 4, DT,
            lambda kt, mb: kv_w[kt][:, mb * 128:(mb + 1) * 128],
            lambda kt, tc: xstr[kt][:, tc * 512:(tc + 1) * 512],
            evac_k, pre_wait=[kw_ld])
    cast_load_stripes(b, kv_w, wv_in)
    vw_ld = ("qg", b.cnt["qg"])

    def evac_v(mb, tc, psum):
        @b.block.vector
        def _(v_, mb=mb, psum=psum, pe=b.cnt["pe"]):
            v_.wait_ge(b.pe, pe)
            b.inc(v_.tensor_copy(out=v_tm[:, mb * 512:(mb + 1) * 512], in_=psum[:]),
                  "ve")
        return ("ve", b.cnt["ve"])

    mm_grid(b, 16, 1, DT,
            lambda kt, mb: xstr[kt][:, mb * 128:(mb + 1) * 128],
            lambda kt, tc: kv_w[kt][:, 0:512],
            evac_v, pre_wait=[vw_ld])
    b.pop()
    b.pop()

    b.push()
    att = b.sb("at_att", [128, NH * T], BF16)
    b.push()
    if l1sel is not None:
        queue_selects(b, "p1", l1sel[0], scol_map=l1sel[1])
    exps = [b.sb("at_exps", [128, 1024], BF16) for _ in range(2)]
    avt = b.sb("at_avt", [128, 1024], BF16)
    rc_row = b.sb("at_rc", [1, 1024], F32)
    ps_s = [b.ps("at_pss", [128, 1024]) for _ in range(2)]
    ps_av = b.ps("at_psav", [128, 1024])
    ps_n = b.ps("at_psn", [1, 1024])
    att_done = []
    b.att_last_ve = b.cnt["ve"]

    for h in range(NH):
        kv = h // 4
        j = h % 2

        @b.block.sync
        def _(q, h=h, j=j):
            if h >= 2:
                sname, v = att_done[h - 2]
                q.wait_ge(getattr(b, sname), v)
            b.inc(q.dma_start(out=qt[j][:], in_=q_rows[h]), "qs", 16)
        q_ld = b.cnt["qs"]

        for qh in range(2):
            prev_ve = b.att_last_ve
            pe0 = b.cnt["pe"]
            ac0 = b.cnt["ac"]

            def score_mm(t, kb, pj):
                last = None
                for ncq in range(2):
                    last = t.matmul(
                        out=ps_s[pj][:, ncq * 512:(ncq + 1) * 512],
                        lhsT=k_fm[:, kv * T + kb * 128: kv * T + (kb + 1) * 128],
                        rhs=qt[j][:, qh * 1024 + ncq * 512: qh * 1024 + (ncq + 1) * 512],
                        start=True, stop=True)
                b.inc(last, "pe")

            def av_mm(t, kb, pj):
                last = None
                for ncq in range(2):
                    t.matmul(out=ps_av[:, ncq * 512:(ncq + 1) * 512],
                             lhsT=v_tm[:, kb * 512 + kv * 128: kb * 512 + (kv + 1) * 128],
                             rhs=exps[pj][:, ncq * 512:(ncq + 1) * 512],
                             start=(kb == 0), stop=(kb == 15))
                    last = t.matmul(out=ps_n[:, ncq * 512:(ncq + 1) * 512],
                                    lhsT=onesb[:],
                                    rhs=exps[pj][:, ncq * 512:(ncq + 1) * 512],
                                    start=(kb == 0), stop=(kb == 15))
                b.inc(last, "pe")

            @b.block.tensor
            def _(t, q_ld=q_ld, prev_ve=prev_ve, pe0=pe0, ac0=ac0, qh=qh):
                for kb in range(16):
                    if kb == 0 and qh == 0:
                        t.wait_ge(b.qs, q_ld)
                    if kb <= 1:
                        t.wait_ge(b.ve, prev_ve)
                    if kb >= 2:
                        t.wait_ge(b.ac, ac0 + kb - 1)
                    score_mm(t, kb, kb % 2)
                    if kb >= 1:
                        t.wait_ge(b.ac, ac0 + kb)
                        av_mm(t, kb - 1, (kb - 1) % 2)
                t.wait_ge(b.ac, ac0 + 16)
                av_mm(t, 15, 1)

            @b.block.scalar
            def _(sc, pe0=pe0, ac0=ac0):
                for kb in range(16):
                    sc.wait_ge(b.pe, pe0 + (kb + 1 if kb <= 1 else 2 * kb))
                    b.inc(sc.activation(out=exps[kb % 2][:], in_=ps_s[kb % 2][:],
                                        func=AF.Exp), "ac")
            assert b.cnt["pe"] == pe0 + 32 and b.cnt["ac"] == ac0 + 16

            @b.block.vector
            def _(v, pe=b.cnt["pe"]):
                v.wait_ge(b.pe, pe)
                b.inc(v.reciprocal(out=rc_row[:], in_=ps_n[:]), "ve")
                v.wait_ge(b.ve, b.cnt["ve"])
                b.inc(v.tensor_copy(out=avt[:], in_=ps_av[:]), "ve")

            @b.block.tensor
            def _(t, ve=b.cnt["ve"]):
                t.wait_ge(b.ve, ve)
                last = None
                for ncq in range(2):
                    last = t.matmul(out=ps_s[1][:, ncq * 512:(ncq + 1) * 512],
                                    lhsT=onesr[:],
                                    rhs=rc_row[:, ncq * 512:(ncq + 1) * 512],
                                    start=True, stop=True)
                b.inc(last, "pe")

            @b.block.vector
            def _(v, h=h, qh=qh, pe=b.cnt["pe"]):
                v.wait_ge(b.pe, pe)
                b.inc(v.tensor_mul(
                    out=att[:, h * T + qh * 1024: h * T + (qh + 1) * 1024],
                    in0=avt[:], in1=ps_s[1][:]), "ve")
            b.att_last_ve = b.cnt["ve"]
        att_done.append(("ve", b.cnt["ve"]))
        b.pump(7)

    b.pump_drain()
    b.pop()
    att_ms = ("ve", b.cnt["ve"])

    wstr2 = [b.sb("at_w2", [128, D], BF16) for _ in range(DT)]
    cast_load_stripes(b, wstr2, wo_in)
    wo_ld = ("qg", b.cnt["qg"])
    os_ = [b.sb("at_o", [128, 512], BF16) for _ in range(2)]

    def evac_o(mb, tc, psum):
        oj = (mb * 4 + tc) % 2

        @b.block.vector
        def _(v_, mb=mb, tc=tc, oj=oj, psum=psum, pe=b.cnt["pe"], n=mb * 4 + tc):
            v_.wait_ge(b.pe, pe)
            if n >= 2:
                v_.wait_ge(b.qg, b.cnt["qg"] - 16)
            b.inc(v_.tensor_copy(out=os_[oj][:], in_=psum[:]), "ve")

        @b.block.gpsimd
        def _(g, mb=mb, tc=tc, oj=oj, ve=b.cnt["ve"]):
            g.wait_ge(b.ve, ve)
            b.inc(g.dma_start(out=s_rows[mb][:, tc * 512:(tc + 1) * 512],
                              in_=os_[oj][:]), "qg", 16)
        return ("ve", b.cnt["ve"])

    mm_grid(b, DT, 4, DT,
            lambda kt, mb: wstr2[kt][:, mb * 128:(mb + 1) * 128],
            lambda kt, tc: att[:, kt * T + tc * 512: kt * T + (tc + 1) * 512],
            evac_o, pre_wait=[wo_ld, att_ms])
    b.pop()
    b.pop()



def build_nc(upto=99, dbg=()):
    b = Bld()
    nc = b.nc
    dbg = set(dbg)

    x_in = b.dr("x", [T, D], F32, kind="ExternalInput")
    win = b.dr("proj_in_w", [D, D], F32, kind="ExternalInput")
    bin_v = b.dr("proj_in_b", [D], F32, kind="ExternalInput")
    wout = b.dr("proj_out_w", [D, D], F32, kind="ExternalInput")
    bout_v = b.dr("proj_out_b", [D], F32, kind="ExternalInput")
    a_in = b.dr("A", [D, S], F32, kind="ExternalInput")
    bw_in = b.dr("B_w", [S, D], F32, kind="ExternalInput")
    cw_in = b.dr("C_w", [D, S], F32, kind="ExternalInput")
    dp_v = b.dr("D_param", [D], F32, kind="ExternalInput")
    lut_in = b.dr("lut", [16], F32, kind="ExternalInput")
    gidx = b.dr("gate_idx", [I0, D], I32, kind="ExternalInput")
    uidx = b.dr("up_idx", [I0, D], I32, kind="ExternalInput")
    didx = b.dr("down_idx", [D, I0], I32, kind="ExternalInput")
    wq_in = b.dr("Wq", [D, D], F32, kind="ExternalInput")
    wk_in = b.dr("Wk", [D, 512], F32, kind="ExternalInput")
    wv_in = b.dr("Wv", [D, 512], F32, kind="ExternalInput")
    wo_in = b.dr("Wo", [D, D], F32, kind="ExternalInput")
    wqg = b.dr("wq_gate", [I1, D], I32, kind="ExternalInput")
    sg_v = b.dr("s_gate", [I1], F32, kind="ExternalInput")
    wqu = b.dr("wq_up", [I1, D], I32, kind="ExternalInput")
    su_v = b.dr("s_up", [I1], F32, kind="ExternalInput")
    wqd = b.dr("wq_down", [D, I1], I32, kind="ExternalInput")
    sd_v = b.dr("s_down", [D], F32, kind="ExternalInput")
    out = b.dr("out", [T, D], F32, kind="ExternalOutput")

    d_xn0 = b.dr("d_xn0", [T, D], BF16)
    d_xnf0 = b.dr("d_xnf0", [D, T], BF16)
    d_xnf1 = b.dr("d_xnf1", [D, T], BF16)
    d_xp = b.dr("d_xp", [D, T], BF16)
    d_y0 = b.dr("d_y0", [D, T], BF16)
    d_s0 = b.dr("d_s0", [D, T], BF16)
    d_u0 = b.dr("d_u0", [D, T], BF16)
    d_x1 = b.dr("d_x1", [T, D], F32)
    d_xn1 = b.dr("d_xn1", [T, D], BF16)
    d_q = b.dr("d_q", [D, T], BF16)
    d_s1 = b.dr("d_s1", [D, T], BF16)
    d_u1 = b.dr("d_u1", [D, T], BF16)
    d_zpad = b.dr("d_zpad", [4, 128, ZROW], BF16)

    sb_bin = b.sb("c_bin", [128, DT], F32)
    sb_bout = b.sb("c_bout", [128, DT], F32)
    sb_dp = b.sb("c_dp", [128, DT], F32)
    sb_sg = b.sb("c_sg", [128, NB1], F32)
    sb_sgm = b.sb("c_sgm", [128, NB1], F32)
    sb_su = b.sb("c_su", [128, NB1], F32)
    sb_sum = b.sb("c_sum", [128, NB1], F32)
    sb_sd = b.sb("c_sd", [128, DT], F32)
    sb_sdm = b.sb("c_sdm", [128, DT], F32)
    sb_lutrow = b.sb("c_lutrow", [1, 16], F32)
    sb_lut128 = b.sb("c_lut128", [128, 16], F32)
    sb_onesb = b.sb("c_onesb", [128, 1], BF16)
    sb_onesr = b.sb("c_onesr", [1, 128], F32)
    sb_ident = b.sb("c_ident", [128, 128], BF16)

    block = b.es.enter_context(nc.Block())
    b.block = block

    load_vec_cols(b, sb_bin, bin_v, DT)
    load_vec_cols(b, sb_bout, bout_v, DT)
    load_vec_cols(b, sb_dp, dp_v, DT)
    load_vec_cols(b, sb_sg, sg_v, NB1)
    load_vec_cols(b, sb_su, su_v, NB1)
    load_vec_cols(b, sb_sd, sd_v, DT)

    @block.sync
    def _(q):
        b.inc(q.dma_start(out=sb_lutrow[:],
                          in_=AP(lut_in.tensor, 0, [[16, 1], [1, 16]])), "qs", 16)

    @block.vector
    def _(v, qs=b.cnt["qs"]):
        v.wait_ge(b.qs, qs)
        b.inc(v.memset(sb_onesb[:], 1.0), "ve")
        b.inc(v.memset(sb_onesr[:], 1.0), "ve")
        b.inc(v.tensor_scalar_mul(out=sb_sgm[:], in0=sb_sg[:], scalar1=-128.0), "ve")
        b.inc(v.tensor_scalar_mul(out=sb_sum[:], in0=sb_su[:], scalar1=-128.0), "ve")
        b.inc(v.tensor_scalar_mul(out=sb_sdm[:], in0=sb_sd[:], scalar1=-128.0), "ve")

    b.push()
    ps_l = b.ps("c_psl", [128, 16])

    @block.tensor
    def _(t, qs=b.cnt["qs"], ve=b.cnt["ve"]):
        t.wait_ge(b.qs, qs)
        t.wait_ge(b.ve, ve)
        b.inc(t.matmul(out=ps_l[:], lhsT=sb_onesr[:], rhs=sb_lutrow[:],
                       start=True, stop=True), "pe")

    @block.vector
    def _(v, pe=b.cnt["pe"]):
        v.wait_ge(b.pe, pe)
        b.inc(v.tensor_copy(out=sb_lut128[:], in_=ps_l[:]), "ve")

    idcolC = b.sb("c_idcol", [128, 128], F32)
    pcolC = b.sb("c_pcol", [128, 1], F32)

    @block.gpsimd
    def _(g):
        g.iota(idcolC[:], pattern=[[1, 128]], base=0, channel_multiplier=0,
               allow_small_or_imprecise_dtypes=True)
        i = g.iota(pcolC[:], pattern=[[0, 1]], base=0, channel_multiplier=1,
                   allow_small_or_imprecise_dtypes=True)
        b.inc(i, "po")

    @block.vector
    def _(v, po=b.cnt["po"]):
        v.wait_ge(b.po, po)
        b.inc(v.tensor_scalar(out=idcolC[:], in0=idcolC[:], scalar1=pcolC[:],
                              scalar2=None, op0=ALU.is_equal), "ve")
        v.wait_ge(b.ve, b.cnt["ve"])
        b.inc(v.tensor_copy(out=sb_ident[:], in_=idcolC[:]), "ve")
    b.pop()

    d_win_rows = b.dr("d_win_rows", [D, D], BF16)
    d_wout_rows = b.dr("d_wout_rows", [D, D], BF16)
    d_bw_rows = b.dr("d_bw_rows", [S, D], BF16)

    b.push()
    ca = [b.sb("cw_c", [128, D], BF16) for _ in range(2)]
    bwrow = b.sb("cw_bw", [16, D], BF16)

    sms = []

    def cast_rows(src, dst):
        st = src.rearrange("(n p) c -> n p c", p=128)
        dt_ = dst.rearrange("(n p) c -> n p c", p=128)
        for i in range(DT):
            @b.block.gpsimd
            def _(g, i=i, nglob=len(sms)):
                if nglob >= 2:
                    sname, v = sms[nglob - 2]
                    g.wait_ge(getattr(b, sname), v)
                b.inc(g.dma_start(out=ca[i % 2][:], in_=st[i]), "qg", 16)

            @b.block.sync
            def _(q, i=i, qg=b.cnt["qg"]):
                q.wait_ge(b.qg, qg)
                b.inc(q.dma_start(out=dt_[i], in_=ca[i % 2][:]), "qs", 16)
            sms.append(("qs", b.cnt["qs"]))

    cast_rows(win, d_win_rows)
    cast_rows(wout, d_wout_rows)

    @b.block.gpsimd
    def _(g):
        b.inc(g.dma_start(out=bwrow[:], in_=bw_in[:, :]), "qg", 16)

    @b.block.sync
    def _(q, qg=b.cnt["qg"]):
        q.wait_ge(b.qg, qg)
        b.inc(q.dma_start(out=d_bw_rows[:, :], in_=bwrow[:]), "qs", 16)
    b.pop()

    if upto >= 2:
        emit_resid_ln(b, "ln0", x_in, None, None, d_xn0)
        emit_fm_copy(b, "xnf0", d_xn0, T, D, d_xnf0)
    if upto >= 6:
        b.push()
        rows0 = queue_selects(b, "p0", [
            ("wg0", gidx, I0, D, "select"),
            ("wu0", uidx, I0, D, "select"),
            ("wd0", didx, D, I0, "select"),
        ], lut128=sb_lut128, ident=sb_ident)
    if upto >= 3:
        emit_proj_in(b, d_win_rows, d_xnf0, sb_bin, d_xp)
    if upto >= 4:
        emit_scan(b, bw_in, a_in, cw_in, sb_dp, sb_onesr, d_bw_rows, d_xp,
                  d_zpad, d_y0)
    if upto >= 5:
        emit_proj_out(b, d_wout_rows, d_y0, sb_bout, d_xnf0, d_s0)
    if upto >= 6:
        b.pump_drain()
        b.pop()
        tl0 = emit_tilings(b, "t0", [
            ("wg0", rows0["wg0"], I0, D),
            ("wu0", rows0["wu0"], I0, D),
        ])
        d_wgT0, d_wuT0 = tl0["wg0"], tl0["wu0"]
        tl0d = emit_tilings(b, "t0d", [("wd0", rows0["wd0"], D, I0)])
        d_wdT0 = tl0d["wd0"]
    if upto >= 7:
        emit_ffn(b, "f0", d_wgT0, d_wuT0, d_wdT0, NB0, d_xnf0, d_s0, d_u0,
                 down_wait=tl0d["_ms"] if upto >= 6 else None)
    if upto >= 8:
        emit_resid_ln(b, "fin0", x_in, d_u0, d_x1, d_xn1)
        emit_fm_copy(b, "xnf1", d_xn1, T, D, d_xnf1)
    if upto >= 9:
        l1sel = None
        if upto >= 10:
            l1sel = ([("wg1", wqg, I1, D, "dequant"),
                      ("wu1", wqu, I1, D, "dequant"),
                      ("wd1", wqd, D, I1, "dequant")],
                     {"wg1": (sb_sg, sb_sgm), "wu1": (sb_su, sb_sum),
                      "wd1": (sb_sd, sb_sdm)})
        emit_l1_attn(b, d_xnf1, wq_in, wk_in, wv_in, wo_in, sb_onesb, sb_onesr,
                     d_q, d_s1, l1sel=l1sel)
    if upto >= 10:
        tl1 = emit_tilings(b, "t1", [
            ("wg1", b.dram["d_wg1_rows"], I1, D),
            ("wu1", b.dram["d_wu1_rows"], I1, D),
        ])
        d_wgT1, d_wuT1 = tl1["wg1"], tl1["wu1"]
        tl1d = emit_tilings(b, "t1d", [("wd1", b.dram["d_wd1_rows"], D, I1)])
        d_wdT1 = tl1d["wd1"]
    if upto >= 11:
        emit_ffn(b, "f1", d_wgT1, d_wuT1, d_wdT1, NB1, d_xnf1, d_s1, d_u1,
                 down_wait=tl1d["_ms"] if upto >= 10 else None)
    if upto >= 12:
        emit_resid_ln(b, "fin1", d_x1, d_u1, out, None)

    dbg_specs = {
        "xn0": ("d_xn0", [T, D], BF16), "xp": ("d_xp", [D, T], BF16),
        "y0": ("d_y0", [D, T], BF16), "s0": ("d_s0", [D, T], BF16),
        "u0": ("d_u0", [D, T], BF16), "x1": ("d_x1", [T, D], F32),
        "xn1": ("d_xn1", [T, D], BF16), "q": ("d_q", [D, T], BF16),
        "s1": ("d_s1", [D, T], BF16), "u1": ("d_u1", [D, T], BF16),
        "wg0": ("d_wg0_rows", [I0, D], BF16),
        "wd0": ("d_wd0_rows", [D, I0], BF16),
        "wg1": ("d_wg1_rows", [I1, D], BF16),
        "zpad": ("d_zpad", [4, 128, ZROW], BF16),
    }
    emit = [(k, v) for k, v in dbg_specs.items() if k in dbg and v[0] in b.dram]
    outs = {}
    for k, (srcn, shape, dt) in emit:
        outs[k] = b.dr("o_" + k, shape, dt, kind="ExternalOutput")

    @block.sync
    def _(q):
        for name, v in b.cnt.items():
            if v > 0:
                q.wait_ge(getattr(b, name), v)
        for k, (srcn, shape, dt) in emit:
            b.inc(q.dma_start(out=outs[k], in_=b.dram[srcn]), "qs", 16)
        q.wait_ge(b.qs, b.cnt["qs"])

    b.barrier()
    return b



INPUT_MAP = {
    "proj_in_w": "L0_attn_proj_in_w", "proj_in_b": "L0_attn_proj_in_b",
    "proj_out_w": "L0_attn_proj_out_w", "proj_out_b": "L0_attn_proj_out_b",
    "A": "L0_attn_A", "B_w": "L0_attn_B_w", "C_w": "L0_attn_C_w",
    "D_param": "L0_attn_D_param", "lut": "L0_ffn_lut",
    "gate_idx": "L0_ffn_gate_idx", "up_idx": "L0_ffn_up_idx",
    "down_idx": "L0_ffn_down_idx",
    "Wq": "L1_attn_Wq", "Wk": "L1_attn_Wk", "Wv": "L1_attn_Wv",
    "Wo": "L1_attn_Wo",
    "wq_gate": "L1_ffn_wq_gate", "s_gate": "L1_ffn_s_gate",
    "wq_up": "L1_ffn_wq_up", "s_up": "L1_ffn_s_up",
    "wq_down": "L1_ffn_wq_down", "s_down": "L1_ffn_s_down",
}


def make_in_maps(inputs):
    shared = {k: np.ascontiguousarray(inputs[v]) for k, v in INPUT_MAP.items()}
    x = np.ascontiguousarray(inputs["L0_attn_x"])
    maps = []
    for c in range(8):
        m = dict(shared)
        m["x"] = np.ascontiguousarray(x[c])
        maps.append(m)
    return maps


def kernel(**inputs):
    b = build_nc()
    res = run_bass_kernel_spmd(b.nc, make_in_maps(inputs), core_ids=list(range(8)))
    return np.stack([np.asarray(r["out"], dtype=np.float32) for r in res.results])
